# version 15
# baseline (speedup 1.0000x reference)
import math
import numpy as np
import ml_dtypes
import concourse.bass as bass
import concourse.mybir as mybir
from concourse.bass_utils import run_bass_kernel_spmd

F32 = mybir.dt.float32
BF16 = mybir.dt.bfloat16
I32 = mybir.dt.int32
AF = mybir.ActivationFunctionType
ALU = mybir.AluOpType
AX = mybir.AxisListType

NCORE = 8
BPC = 2
T = 2048
D = 1024
NCH = 8
H = 4
M = 64
HK = 128
HV = 256
NTB = 16
NTT = 4
SCALE = HK ** -0.5
ENGS = ("sync", "tensor", "vector", "scalar", "gpsimd")
TAU = 2.0 * math.pi


def _cw_consts():
    def trunc12(x):
        u = np.float32(x).view(np.uint32)
        u = np.uint32(int(u) & ~0xFFF)
        return float(u.view(np.float32))
    c1 = trunc12(TAU)
    c2 = trunc12(TAU - c1)
    c3 = float(np.float32(TAU - c1 - c2))
    return c1, c2, c3


class Sem:
    __slots__ = ("h", "n")

    def __init__(self, nc, name):
        self.h = nc.alloc_semaphore(name)
        self.n = 0

    def now(self):
        return (self, self.n)


class Plan:

    def __init__(self, nc, serialize=False):
        self.nc = nc
        self.serialize = serialize
        self.q = {e: [] for e in ENGS}
        self.bar = Sem(nc, "bar")
        self._last_out = {}
        self.dma_sems = []
        self.dma_pool = []
        self.retired = set()
        self.eng_sem = {}
        self.all_sems = []
        self.wmark = {}
        self.rmark = {}
        self.hwm = {e: {} for e in ENGS}
        self.guards = {}

    def add(self, eng, call=None, waits=(), inc=None, k=1):
        w = []
        hw = self.hwm[eng]
        own = self.eng_sem.get(eng) if eng != "sync" else None
        for s, v in waits:
            v = int(v)
            if v <= 0 or s is own or s in self.retired:
                continue
            if hw.get(s, 0) >= v:
                continue
            hw[s] = v
            w.append((s.h, v))
        ih = None
        if inc is not None:
            ih = inc.h
            inc.n += k
        self.q[eng].append((tuple(w), call, ih, k))

    def _fix_dma(self, m):
        s, v = m
        if s in self._dma_set:
            return (s, s.n)
        return m

    def _dep_waits(self, ins, outs, eng=None):
        if self.serialize:
            return [s.now() for s in self.all_sems]
        ws = []
        for nm in ins:
            m = self.wmark.get(nm)
            if m is not None:
                ws.append(self._fix_dma(m))
            if nm.startswith("pb"):
                for e2, m2 in self.rmark.get(nm, {}).items():
                    if e2 != eng:
                        ws.append(self._fix_dma(m2))
        for nm in outs:
            m = self.wmark.get(nm)
            if m is not None:
                ws.append(self._fix_dma(m))
            for m in self.rmark.get(nm, {}).values():
                ws.append(self._fix_dma(m))
        return ws

    def _dep_update(self, eng, ins, outs, sem, val):
        for nm in outs:
            self.wmark[nm] = (sem, val)
            self.rmark[nm] = {}
        for nm in ins:
            self.rmark.setdefault(nm, {})[eng] = (sem, val)

    def op(self, eng, name, *args, waits=(), inc=None, k=1, **kwargs):
        outs = set()
        ins = set()
        for j, a in enumerate(args):
            nm = getattr(getattr(a, "tensor", None), "name", None)
            if nm is None:
                continue
            (outs if j == 0 else ins).add(nm)
        for kk, a in kwargs.items():
            nm = getattr(getattr(a, "tensor", None), "name", None)
            if nm is None:
                continue
            (outs if kk == "accum_out" else ins).add(nm)
        if eng in ("vector", "scalar", "gpsimd"):
            hist = self._last_out.setdefault(eng, [])
            if any(ins & h for h in hist):
                self.add(eng, ("drain", (), {}))
                hist.clear()
            hist.append(outs)
            del hist[:-2]
        if inc is None:
            inc = self.eng_sem[eng]
            k = 1
        autow = self._dep_waits(ins, outs, eng)
        self.add(eng, (name, args, kwargs), waits=tuple(waits) + tuple(autow),
                 inc=inc, k=k)
        self._dep_update(eng, ins, outs, inc, inc.n)

    def dma(self, eng, out, in_, sem, waits=(), transpose=False, slow=False):
        kw = {"out": out, "in_": in_}
        if transpose:
            kw["transpose"] = True
        if slow:
            kw["allow_slow_non_contiguous"] = True
        ins = set()
        outs = set()
        onm = getattr(getattr(out, "tensor", None), "name", None)
        if onm:
            outs.add(onm)
        nm = getattr(getattr(in_, "tensor", None), "name", None)
        if nm:
            ins.add(nm)
        if self.dma_pool:
            key = onm or nm or "x"
            sem = self.dma_pool[hash(key) % len(self.dma_pool)]
        autow = self._dep_waits(ins, outs, eng)
        self.add(eng, ("dma_start", (), kw), waits=tuple(waits) + tuple(autow),
                 inc=sem, k=16)
        self._dep_update(eng, ins, outs, sem, sem.n)

    def g(self, key):
        return self.guards.get(key, ())

    def setg(self, key, *gs):
        self.guards[key] = tuple(gs)

    def barrier(self, force=False):
        if not force and getattr(self, "no_barriers", False):
            self.guards.clear()
            return
        dws = tuple(s.now() for s in self.dma_sems)
        for e in ENGS:
            for s, v in dws:
                if v > 0:
                    self.add(e, None, waits=((s, v),))
        tgt = self.bar.n + len(ENGS)
        for e in ENGS:
            self.add(e, ("nop", (), {}), inc=self.bar)
        for e in ENGS:
            self.add(e, None, waits=((self.bar, tgt),))
        allsems = set(self.dma_sems) | set(self.eng_sem.values())
        for m in self.wmark.values():
            allsems.add(m[0])
        for e in ENGS:
            for s in allsems:
                self.hwm[e][s] = max(self.hwm[e].get(s, 0), s.n)
        self.guards.clear()

    def emit(self):
        nc = self.nc
        with nc.Block() as block:
            def mk(eng_name):
                def body(eng):
                    for w, call, ih, k in self.q[eng_name]:
                        for hh, v in w:
                            eng.wait_ge(hh, v)
                        if call is None:
                            continue
                        name, args, kwargs = call
                        ins = getattr(eng, name)(*args, **kwargs)
                        if ih is not None:
                            ins.then_inc(ih, k)
                return body
            block.sync(mk("sync"))
            block.tensor(mk("tensor"))
            block.vector(mk("vector"))
            block.scalar(mk("scalar"))
            block.gpsimd(mk("gpsimd"))


def dap(handle, offset, dims):
    return bass.AP(handle, int(offset), [list(d) for d in dims])


def build(upto=99, taps=(), serialize=False, ser_mask=0):
    nc = bass.Bass("TRN2", target_bir_lowering=False)
    C1, C2, C3 = _cw_consts()

    def din(name, shape, dt=F32):
        return nc.dram_tensor(name, list(shape), dt, kind="ExternalInput")

    x0_t = din("x0", (BPC, NCH, 128, T))
    wq_t = din("wq", (D, 512)); wk_t = din("wk", (D, 512))
    wv_t = din("wv", (D, D)); ws_t = din("ws", (D, 256))
    wg_t = din("wg", (D, D)); wo_t = din("wo", (D, D))
    qcw_t = din("qcw", (4, 128, 4)); kcw_t = din("kcw", (4, 128, 4))
    vcw_t = din("vcw", (8, 128, 4))
    gnw_t = din("gnw", (128, 256))
    wlin_t = din("wlin", (D, D)); wconv_t = din("wconv", (D, D))
    wlin2_t = din("wlin2", (D, D)); wa_t = din("wa", (D, D)); wx_t = din("wx", (D, D))
    w1_t = din("w1", (D, D)); w2_t = din("w2", (D, D)); w3_t = din("w3", (D, D))
    bcols_t = din("bcols", (8, NCH, 128, 1))
    g1_t = din("g1", (NCH, 128, 1)); g2_t = din("g2", (NCH, 128, 1))
    lam_t = din("lam", (NCH, 128, 1))
    t1r_t = din("t1r", (4, 512, 512)); t2r_t = din("t2r", (4, 512, 512))
    t1i_t = din("t1i", (4, 512, 512)); t2i_t = din("t2i", (4, 512, 512))
    fbr_t = din("fbr", (4, 128, 1)); fbi_t = din("fbi", (4, 128, 1))
    tri_t = din("tri", (128, 128))
    triq_t = din("triq", (128, 128))
    i64f_t = din("i64f", (128, 64))
    i128b_t = din("i128b", (128, 128), BF16)
    rmat_t = din("rmat", (128, 128), BF16)
    ones_t = din("onesb", (128, 1), BF16)

    out_t = nc.dram_tensor("out", [BPC, NCH, 128, T], F32, kind="ExternalOutput")

    def dsc(name, shape, dt=F32):
        return nc.dram_tensor(name, list(shape), dt, kind="Internal")

    dX1 = dsc("dX1", (BPC, NCH, 128, T))
    dX2 = dsc("dX2", (BPC, NCH, 128, T))
    dX3 = dsc("dX3", (BPC, NCH, 128, T))
    dY3 = dsc("dY3", (BPC, NCH, 128, T))
    dY3c = dsc("dY3c", (BPC, NCH, 128, T), BF16)
    dVfm = dsc("dVfm", (BPC, NCH, 128, T), BF16)
    dKfm = dsc("dKfm", (BPC, H, 128, T), BF16)
    dEs = dsc("dEs", (BPC, 2, 128, T), BF16)
    dGtm = dsc("dGtm", (BPC, NTB, 128, D), BF16)
    dOtm = dsc("dOtm", (BPC, H, NTB, 128, HV), BF16)
    dLin = dsc("dLin", (BPC, NCH, 128, T), BF16)
    dXgel = dsc("dXgel", (BPC, NCH, 128, T), BF16)
    dXrg = dsc("dXrg", (BPC, NCH, 128, T))
    dXrgb = dsc("dXrgb", (BPC, NCH, 128, T), BF16)
    dLa = dsc("dLa", (BPC, NCH, 128, T))
    dXgh = dsc("dXgh", (BPC, NCH, 128, T), BF16)
    dResB = dsc("dResB", (BPC, NCH, 128, T))
    dGate = dsc("dGate", (BPC, NCH, 128, T), BF16)
    dComb = dsc("dComb", (BPC, NCH, 128, T), BF16)
    dXab = dsc("dXab", (BPC, H, 128, NTB * M))
    dRows = dsc("dRows", (BPC, 8192))
    dRows2 = dsc("dRows2", (BPC, 8192))

    tap_map = {"x1": dX1, "x2": dX2, "x3": dX3, "y3": dY3, "y3c": dY3c, "vfm": dVfm,
               "kfm": dKfm, "es": dEs, "gtm": dGtm, "otm": dOtm, "lin": dLin,
               "xgel": dXgel, "xrg": dXrg, "la": dLa, "xgh": dXgh, "resb": dResB,
               "gate": dGate, "comb": dComb}
    tap_outs = {}
    for tp in taps:
        if tp in ("y1", "reps", "rows", "rows2", "reps3"):
            continue
        srct = tap_map[tp]
        tap_outs[tp] = (srct, nc.dram_tensor(
            "tap_" + tp, list(srct.shape), srct.dtype, kind="ExternalOutput"))
    dbg_y1 = dbg_reps = dbg_rows = None
    if "y1" in taps:
        dbg_y1 = nc.dram_tensor("tap_y1", [BPC, NCH, 128, T], BF16, kind="ExternalOutput")
    if "reps" in taps:
        dbg_reps = nc.dram_tensor("tap_reps", [BPC, 2, 128, T], F32, kind="ExternalOutput")
    if "rows" in taps:
        dbg_rows = nc.dram_tensor("tap_rows", [BPC, 8192], F32, kind="ExternalOutput")
    dbg_reps3 = None
    if "reps3" in taps:
        dbg_reps3 = nc.dram_tensor("tap_reps3", [BPC, 3, 128, T], F32,
                                   kind="ExternalOutput")
    dbg_rows2 = None
    if "rows2" in taps:
        dbg_rows2 = nc.dram_tensor("tap_rows2", [BPC, 3, 8192], F32, kind="ExternalOutput")

    def sb(name, shape, dt=F32):
        return nc.alloc_sbuf_tensor("sb_" + name, list(shape), dt).ap()

    tri = sb("tri", (128, 128))
    triq = sb("triq", (128, 128))
    i64f = sb("i64f", (128, 64))
    i128b = sb("i128b", (128, 128), BF16)
    rmat = sb("rmat", (128, 128), BF16)
    onesb = sb("onesb", (128, 1), BF16)
    gnw = sb("gnw", (128, 256))
    qcw = sb("qcw", (128, 16)); kcw = sb("kcw", (128, 16)); vcw = sb("vcw", (128, 32))
    bcols = sb("bcols", (128, 64))
    g1c = sb("g1c", (128, 8)); g2c = sb("g2c", (128, 8))
    logac = sb("logac", (128, 8))
    fbr = sb("fbr", (128, 4)); fbi = sb("fbi", (128, 4))
    cosb = sb("cosb", (128, T), BF16); sinb = sb("sinb", (128, T), BF16)
    wrt = [sb(f"wrt{c}", (128, 512), BF16) for c in range(4)]
    wit = [sb(f"wit{c}", (128, 512), BF16) for c in range(4)]
    nwit = [sb(f"nwit{c}", (128, 512), BF16) for c in range(4)]

    xch = [sb(f"xch{i}", (128, T)) for i in range(2)]
    tmpA = sb("tmpA", (128, T))
    tmpB = sb("tmpB", (128, T))
    repA = sb("repA", (128, T))
    repB = sb("repB", (128, T))
    ybf = [sb(f"ybf{i}", (128, T), BF16) for i in range(NCH)]
    wbf = [sb(f"wbf{i}", (128, 1024), BF16) for i in range(NCH)]
    wstg = [sb(f"wstg{i}", (128, 512)) for i in range(2)]
    blk = sb("blk", (128, 64))
    blk2 = sb("blk2", (128, 64))
    blk3 = sb("blk3", (128, 64))

    qfm = [sb(f"qfm{h}", (128, T), BF16) for h in range(H)]
    sfp = sb("sfp", (128, T))
    repC = sfp
    esm_h = sb("esm_h", (64, T), BF16)
    xabt = sb("xabt", (128, NTB * M))
    convb = sb("convb", (128, T + 4), BF16)
    qsil = [sb(f"qsil{i}", (128, T), BF16) for i in range(2)]
    xbst = [qsil[0][:, 0:512], qsil[0][:, 512:1024]]
    xsqst = [qsil[1][:, 0:512], qsil[1][:, 512:1024]]
    rb1 = tmpA[0:1, 0:2048]
    rb2 = tmpB[0:1, 0:2048]

    atm = sb("atm", (128, 128), BF16)
    kch = [sb(f"kch{i}", (128, 128), BF16) for i in range(2)]
    eraw = [sb(f"eraw{i}", (64, 128), BF16) for i in range(2)]
    vraw = [sb(f"vraw{i}", (128, 256), BF16) for i in range(2)]
    wtm = sb("wtm", (128, 128), BF16)
    estm = [sb(f"estm{i}", (128, M), BF16) for i in range(2)]
    ktm = [sb(f"ktm{i}", (128, 128), BF16) for i in range(2)]
    vtm = [sb(f"vtm{i}", (128, HV), BF16) for i in range(2)]
    gsil = [sb(f"gsil{i}", (128, HV), BF16) for i in range(2)]
    okt = sb("okt", (128, M))
    eok = sb("eok", (128, M))
    sumc = sb("sumc", (128, 1))
    rsumc = sb("rsumc", (128, 1))
    qvxa = sb("qvxa", (128, M), BF16)
    qvxat = sb("qvxat", (64, 128), BF16)
    s1f = sb("s1f", (128, M)); s1b = sb("s1b", (128, M), BF16)
    s2f = sb("s2f", (64, HV)); s2b = sb("s2b", (64, HV), BF16)
    s1alt = blk[:, 0:64]
    s2alt = tmpA[0:64, 1024:1280]
    ssqc = sb("ssqc", (128, 1))
    rscc = sb("rscc", (128, 1))
    scr256 = sfp[:, 256:512]
    o1t = sfp[:, 0:256]
    o2t = sb("o2t", (128, HV), BF16)
    otf = [convb[:, 0:512], convb[:, 512:1024]]

    pb = [nc.alloc_psum_tensor(f"pb{i}", [128, 512], F32).ap() for i in range(8)]

    pl = Plan(nc, serialize=serialize)
    pl.ser_base = serialize
    pl.no_barriers = True

    def ser(bit):
        pl.serialize = pl.ser_base or bool(ser_mask & (1 << bit))
    s_ld = Sem(nc, "s_ld")
    s_st = Sem(nc, "s_st")
    s_pe = Sem(nc, "s_pe")
    s_dv = Sem(nc, "s_dv")
    s_ac = Sem(nc, "s_ac")
    s_gp = Sem(nc, "s_gp")
    s_tl = Sem(nc, "s_tl")
    pl.dma_pool = [Sem(nc, f"dp{i}") for i in range(32)]
    pl.dma_sems = list(pl.dma_pool)
    pl.retired = {s_ld, s_st, s_tl}
    pl.eng_sem = {"tensor": s_pe, "vector": s_dv, "scalar": s_ac, "gpsimd": s_gp}
    pl.all_sems = pl.dma_sems + [s_pe, s_dv, s_ac, s_gp]
    pl._dma_set = set(pl.dma_sems)

    def dchunk(t, b, ch, tt=None):
        off = (b * NCH + ch) * 128 * T
        if tt is None:
            return dap(t, off, [[T, 128], [1, T]])
        return dap(t, off + tt * 512, [[T, 128], [1, 512]])

    pl.dma("sync", tri, tri_t.ap(), s_ld)
    pl.dma("sync", triq, triq_t.ap(), s_ld)
    pl.dma("sync", i64f, i64f_t.ap(), s_ld)
    pl.dma("sync", i128b, i128b_t.ap(), s_ld)
    pl.dma("sync", rmat, rmat_t.ap(), s_ld)
    pl.dma("sync", onesb, ones_t.ap(), s_ld)
    pl.dma("sync", gnw, gnw_t.ap(), s_ld)
    pl.dma("sync", qcw, dap(qcw_t, 0, [[4, 128], [512, 4], [1, 4]]), s_ld, slow=True)
    pl.dma("sync", kcw, dap(kcw_t, 0, [[4, 128], [512, 4], [1, 4]]), s_ld, slow=True)
    pl.dma("sync", vcw, dap(vcw_t, 0, [[4, 128], [512, 8], [1, 4]]), s_ld, slow=True)
    pl.dma("sync", bcols, dap(bcols_t, 0, [[1, 128], [1024, 8], [128, 8]]), s_ld, slow=True)
    pl.dma("sync", g1c, dap(g1_t, 0, [[1, 128], [128, 8]]), s_ld, slow=True)
    pl.dma("sync", g2c, dap(g2_t, 0, [[1, 128], [128, 8]]), s_ld, slow=True)
    pl.dma("sync", fbr, dap(fbr_t, 0, [[1, 128], [128, 4]]), s_ld, slow=True)
    pl.dma("sync", fbi, dap(fbi_t, 0, [[1, 128], [128, 4]]), s_ld, slow=True)
    pl.dma("sync", blk[:, 0:8], dap(lam_t, 0, [[1, 128], [128, 8]]), s_ld, slow=True)
    W0 = (s_ld.now(),)

    pl.op("scalar", "activation", blk2[:, 0:8], blk[:, 0:8], AF.Exp, scale=-1.0,
          waits=W0, inc=s_ac)
    pl.op("scalar", "activation", blk3[:, 0:8], blk2[:, 0:8], AF.Ln, bias=1.0, inc=s_ac)
    pl.op("vector", "tensor_scalar_mul", logac[:, 0:8], blk3[:, 0:8], -0.125,
          waits=(s_ac.now(),), inc=s_dv)

    kint = repA.bitcast(I32)
    kflt = repB
    pl.op("gpsimd", "iota", tmpA, [[1, T]], channel_multiplier=0,
          allow_small_or_imprecise_dtypes=True, inc=s_gp)
    pl.op("gpsimd", "iota", blk[0:64, 8:9], [[0, 1]], channel_multiplier=1,
          allow_small_or_imprecise_dtypes=True, inc=s_gp)
    pl.op("gpsimd", "iota", blk[64:128, 8:9], [[0, 1]], channel_multiplier=1,
          allow_small_or_imprecise_dtypes=True, inc=s_gp)
    pl.op("scalar", "activation", blk[:, 9:10], blk[:, 8:9], AF.Exp,
          scale=-math.log(1e4) / 64.0, waits=(s_gp.now(),), inc=s_ac)
    pl.op("vector", "tensor_scalar_mul", tmpB, tmpA, blk[:, 9:10],
          waits=(s_ac.now(), s_gp.now()), inc=s_dv)
    pl.op("vector", "tensor_scalar_mul", tmpA, tmpB, 1.0 / TAU, inc=s_dv)
    pl.op("vector", "tensor_copy", kint, tmpA, inc=s_dv)
    pl.op("vector", "tensor_copy", kflt, kint, inc=s_dv)
    pl.op("vector", "scalar_tensor_tensor", tmpA, kflt, -C1, tmpB, ALU.mult, ALU.add, inc=s_dv)
    pl.op("vector", "scalar_tensor_tensor", tmpB, kflt, -C2, tmpA, ALU.mult, ALU.add, inc=s_dv)
    pl.op("vector", "scalar_tensor_tensor", tmpA, kflt, -C3, tmpB, ALU.mult, ALU.add, inc=s_dv)
    pl.op("vector", "tensor_single_scalar", tmpB, tmpA, math.pi, ALU.is_gt, inc=s_dv)
    pl.op("vector", "scalar_tensor_tensor", kflt, tmpB, -TAU, tmpA, ALU.mult, ALU.add, inc=s_dv)
    pl.op("scalar", "activation", tmpA, kflt, AF.Sin, waits=(s_dv.now(),), inc=s_ac)
    pl.op("vector", "tensor_copy", sinb, tmpA, waits=(s_ac.now(),), inc=s_dv)
    pl.op("vector", "tensor_scalar_add", tmpB, kflt, math.pi / 2.0, inc=s_dv)
    pl.op("vector", "tensor_single_scalar", tmpA, tmpB, math.pi, ALU.is_gt, inc=s_dv)
    pl.op("vector", "scalar_tensor_tensor", kflt, tmpA, -TAU, tmpB, ALU.mult, ALU.add, inc=s_dv)
    pl.op("scalar", "activation", tmpA, kflt, AF.Sin, waits=(s_dv.now(),), inc=s_ac)
    pl.op("vector", "tensor_copy", cosb, tmpA, waits=(s_ac.now(),), inc=s_dv)

    for kind in ("r", "i"):
        t1t, t2t, dst = (t1r_t, t2r_t, wrt) if kind == "r" else (t1i_t, t2i_t, wit)
        for ch4 in range(4):
            prods = [tmpA[:, 0:512], tmpB[:, 0:512], tmpA[:, 512:1024], tmpB[:, 512:1024]]
            for c in range(4):
                i = c % 2
                t1s = xch[i][:, 0:512]
                t2s = wstg[i][:, 0:512]
                off = (c * 512 + ch4 * 128) * 512
                pl.dma("sync", t1s, dap(t1t, off, [[512, 128], [1, 512]]), s_ld,
                       waits=pl.g(("ffnw", i)))
                pl.dma("sync", t2s, dap(t2t, off, [[512, 128], [1, 512]]), s_ld)
                pl.op("vector", "tensor_mul", prods[c], t1s, t2s,
                      waits=(s_ld.now(),), inc=s_dv)
                pl.setg(("ffnw", i), s_dv.now())
            pl.op("vector", "tensor_add", wstg[0][:, 0:512], prods[0], prods[1], inc=s_dv)
            pl.op("vector", "tensor_add", wstg[1][:, 0:512], prods[2], prods[3], inc=s_dv)
            pl.setg(("ffnw", 0), s_dv.now())
            pl.setg(("ffnw", 1), s_dv.now())
            pl.op("vector", "tensor_add", tmpA[:, 1024:1536], wstg[0][:, 0:512],
                  wstg[1][:, 0:512], inc=s_dv)
            pl.op("vector", "tensor_copy", dst[ch4], tmpA[:, 1024:1536], inc=s_dv)
            if kind == "i":
                pl.op("vector", "tensor_scalar_mul", nwit[ch4], tmpA[:, 1024:1536],
                      -1.0, inc=s_dv)
    pl.barrier()

    def ln_stats(src_t, b, need=2):
        for tt in range(NTT):
            for ch in range(NCH):
                i = ch % 2
                pl.dma("sync", xch[i][:, 0:512], dchunk(src_t, b, ch, tt), s_ld,
                       waits=pl.g(("st_x", i)))
                WL = (s_ld.now(),)
                if need == 2:
                    pl.op("vector", "tensor_copy", xbst[i], xch[i][:, 0:512],
                          waits=WL, inc=s_dv)
                pl.op("scalar", "activation", xsqst[i], xch[i][:, 0:512], AF.Square,
                      waits=WL, inc=s_ac)
                pl.setg(("st_x", i), s_dv.now(), s_ac.now())
                if need == 2:
                    pl.op("tensor", "matmul", pb[4][0:1, 0:512], onesb, xbst[i],
                          start=(ch == 0), stop=(ch == NCH - 1),
                          waits=(s_dv.now(),) + (pl.g(("st_pb", 4)) if ch == 0 else ()),
                          inc=s_pe)
                pl.op("tensor", "matmul", pb[5][0:1, 0:512], onesb, xsqst[i],
                      start=(ch == 0), stop=(ch == NCH - 1),
                      waits=(s_ac.now(),) + (pl.g(("st_pb", 5)) if ch == 0 else ()),
                      inc=s_pe)
            W = (s_pe.now(),)
            if need == 2:
                pl.op("vector", "tensor_copy", rb1[:, tt * 512:(tt + 1) * 512],
                      pb[4][0:1, 0:512], waits=W, inc=s_dv)
                pl.setg(("st_pb", 4), s_dv.now())
            pl.op("vector", "tensor_copy", rb2[:, tt * 512:(tt + 1) * 512],
                  pb[5][0:1, 0:512], waits=W, inc=s_dv)
            pl.setg(("st_pb", 5), s_dv.now())

    def rows_to_blk(b):
        pl.dma("gpsimd", dap(dRows, b * 8192, [[1, 1], [1, 2048]]),
               rb1, s_st, waits=(s_dv.now(),))
        pl.dma("gpsimd", dap(dRows, b * 8192 + 2048, [[1, 1], [1, 2048]]),
               rb2, s_st)
        pl.dma("sync", blk[:, 0:16], dap(dRows, b * 8192, [[16, 128], [1, 16]]), s_ld,
               waits=(s_st.now(),))
        pl.dma("sync", blk[:, 16:32], dap(dRows, b * 8192 + 2048, [[16, 128], [1, 16]]),
               s_ld)

    def ln_rowmath(b, kind):
        WL = (s_ld.now(),)
        s1 = blk[:, 0:16]
        s2 = blk[:, 16:32]
        if kind == "rms":
            nr = blk2[:, 0:16]
            pl.op("scalar", "activation", nr, s2, AF.Sqrt, waits=WL, inc=s_ac)
            den = blk3[:, 0:16]
            pl.op("vector", "tensor_scalar_max", den, nr, 1e-12,
                  waits=(s_ac.now(),), inc=s_dv)
            lnd = blk2[:, 16:32]
            pl.op("scalar", "activation", lnd, den, AF.Ln, waits=(s_dv.now(),), inc=s_ac)
            r2 = blk3[:, 16:32]
            pl.op("scalar", "activation", r2, lnd, AF.Exp, scale=-1.0, inc=s_ac)
            pl.op("vector", "tensor_scalar_mul", blk2[:, 32:48], r2, 32.0,
                  waits=(s_ac.now(),), inc=s_dv)
            pl.dma("gpsimd", dap(dRows2, b * 8192, [[16, 128], [1, 16]]),
                   blk2[:, 32:48], s_st, waits=(s_dv.now(),))
            pl.dma("sync", repA, dap(dRows2, b * 8192, [[0, 128], [1, 2048]]),
                   s_ld, waits=(s_st.now(),))
            return
        mu = blk3[:, 0:16]
        pl.op("vector", "tensor_scalar_mul", mu, s1, 1.0 / D, waits=WL, inc=s_dv)
        musq = blk2[:, 16:32]
        pl.op("scalar", "activation", musq, mu, AF.Square,
              waits=(s_dv.now(),), inc=s_ac)
        var = blk3[:, 16:32]
        pl.op("vector", "scalar_tensor_tensor", var, s2, 1.0 / D, musq,
              ALU.mult, ALU.subtract, waits=(s_ac.now(),), inc=s_dv)
        varep = blk2[:, 0:16]
        pl.op("vector", "tensor_scalar_add", varep, var, 1e-5, inc=s_dv)
        lnv = blk[:, 32:48]
        pl.op("scalar", "activation", lnv, varep, AF.Ln, waits=(s_dv.now(),), inc=s_ac)
        rstd = blk3[:, 32:48]
        pl.op("scalar", "activation", rstd, lnv, AF.Exp, scale=-0.5, inc=s_ac)
        murs = blk2[:, 32:48]
        pl.op("vector", "tensor_mul", murs, mu, rstd, waits=(s_ac.now(),), inc=s_dv)
        pl.dma("gpsimd", dap(dRows2, b * 8192, [[16, 128], [1, 16]]),
               rstd, s_st, waits=(s_dv.now(),))
        pl.dma("gpsimd", dap(dRows2, b * 8192 + 2048, [[16, 128], [1, 16]]),
               murs, s_st)
        if kind == "ln3":
            sq = blk[:, 48:64]
            pl.op("scalar", "activation", sq, var, AF.Sqrt, scale=float(D), inc=s_ac)
            den = blk2[:, 48:64]
            pl.op("vector", "tensor_mul", den, sq, rstd,
                  waits=(s_ac.now(), s_st.now()), inc=s_dv)
            den2 = blk3[:, 48:64]
            pl.op("vector", "tensor_scalar_max", den2, den, 1e-12, inc=s_dv)
            lnd = blk[:, 0:16]
            pl.op("scalar", "activation", lnd, den2, AF.Ln, waits=(s_dv.now(),), inc=s_ac)
            inv = blk2[:, 0:16]
            pl.op("scalar", "activation", inv, lnd, AF.Exp, scale=-1.0, inc=s_ac)
            rat = blk3[:, 0:16]
            pl.op("vector", "tensor_scalar_mul", rat, inv, 32.0,
                  waits=(s_ac.now(),), inc=s_dv)
            pl.dma("gpsimd", dap(dRows2, b * 8192 + 4096, [[16, 128], [1, 16]]),
                   rat, s_st, waits=(s_dv.now(),))
        WS = (s_st.now(),)
        pl.dma("sync", repA, dap(dRows2, b * 8192, [[0, 128], [1, 2048]]), s_ld, waits=WS)
        pl.dma("sync", repB, dap(dRows2, b * 8192 + 2048, [[0, 128], [1, 2048]]), s_ld)
        if kind == "ln3":
            pl.dma("sync", repC, dap(dRows2, b * 8192 + 4096, [[0, 128], [1, 2048]]), s_ld)

    def weight_to_bf(dram_t, n_out, gcol=None, neg=False):
        for ch in range(NCH):
            for half in range(0, n_out, 512):
                w = min(512, n_out - half)
                i = (ch * 2 + half // 512) % 2
                pl.dma("sync", wstg[i][:, 0:w],
                       dap(dram_t, ch * 128 * n_out + half, [[n_out, 128], [1, w]]),
                       s_ld, waits=pl.g(("wstg", i)))
                WL = (s_ld.now(),)
                if gcol is not None:
                    pl.op("vector", "tensor_scalar_mul", wbf[ch][:, half:half + w],
                          wstg[i][:, 0:w], gcol[:, ch:ch + 1], waits=WL, inc=s_dv)
                elif neg:
                    pl.op("vector", "tensor_scalar_mul", wbf[ch][:, half:half + w],
                          wstg[i][:, 0:w], -1.0, waits=WL, inc=s_dv)
                else:
                    pl.op("vector", "tensor_copy", wbf[ch][:, half:half + w],
                          wstg[i][:, 0:w], waits=WL, inc=s_dv)
                pl.setg(("wstg", i), s_dv.now())

    def ln_apply(src_t, b, dst_list):
        for ch in range(NCH):
            i = ch % 2
            pl.dma("sync", xch[i], dchunk(src_t, b, ch), s_ld, waits=pl.g(("la_x", i)))
            tgt = tmpA if i == 0 else tmpB
            pl.op("gpsimd", "tensor_mul", tgt, xch[i], repA,
                  waits=(s_ld.now(),) + pl.g(("la_t", i)), inc=s_gp)
            pl.setg(("la_x", i), s_gp.now())
            pl.op("vector", "tensor_sub", dst_list[ch], tgt, repB,
                  waits=(s_gp.now(),), inc=s_dv)
            pl.setg(("la_t", i), s_dv.now())

    def proj_ws(bank, wcol, rhs, tt, n_k=NCH, extraw=()):
        for ch in range(n_k):
            w = extraw + (pl.g(("pb", bank)) if ch == 0 else ())
            pl.op("tensor", "matmul", pb[bank][:, 0:512], wcol[ch],
                  rhs[ch][:, tt * 512:(tt + 1) * 512],
                  start=(ch == 0), stop=(ch == n_k - 1), waits=w,
                  inc=s_pe if ch == n_k - 1 else None)

    def conv_silu(cwcol, dst_bf):
        pl.op("vector", "tensor_scalar_mul", tmpA, convb[:, 4:4 + T], cwcol[3],
              inc=s_dv)
        srcs = [convb[:, 3:3 + T], convb[:, 2:2 + T], convb[:, 1:1 + T]]
        bufs = [(tmpA, tmpB), (tmpB, tmpA), (tmpA, tmpB)]
        for j in range(3):
            a, d = bufs[j]
            pl.op("vector", "scalar_tensor_tensor", d, srcs[j], cwcol[2 - j], a,
                  ALU.mult, ALU.add, inc=s_dv)
        pl.op("scalar", "activation", dst_bf, tmpB, AF.Silu,
              waits=(s_dv.now(),), inc=s_ac)

    for b in range(BPC):
        if upto < 1:
            break
        ser(0)
        ln_stats(x0_t, b)
        rows_to_blk(b)
        ln_rowmath(b, "ln")
        pl.barrier()
        if upto < 2:
            continue

        ln_apply(x0_t, b, ybf)
        pl.barrier()
        if dbg_reps is not None:
            pl.dma("gpsimd", dap(dbg_reps, (b * 2 + 0) * 128 * T, [[T, 128], [1, T]]),
                   repA, s_st)
            pl.dma("gpsimd", dap(dbg_reps, (b * 2 + 1) * 128 * T, [[T, 128], [1, T]]),
                   repB, s_st)
        if dbg_rows is not None:
            pass
        if dbg_rows2 is not None:
            pl.dma("gpsimd", dap(dbg_rows2, b * 3 * 8192, [[64, 128], [1, 64]]),
                   blk[:, 0:64], s_st)
            pl.dma("gpsimd", dap(dbg_rows2, b * 3 * 8192 + 8192, [[64, 128], [1, 64]]),
                   blk2[:, 0:64], s_st)
            pl.dma("gpsimd", dap(dbg_rows2, b * 3 * 8192 + 16384, [[64, 128], [1, 64]]),
                   blk3[:, 0:64], s_st)
        if dbg_y1 is not None:
            for ch in range(NCH):
                pl.dma("gpsimd", dap(dbg_y1, (b * NCH + ch) * 128 * T, [[T, 128], [1, T]]),
                       ybf[ch], s_st)
        if dbg_y1 is not None or dbg_reps is not None or dbg_rows is not None:
            pl.barrier()

        for kind in ("q", "k"):
            wt = wq_t if kind == "q" else wk_t
            cw = qcw if kind == "q" else kcw
            weight_to_bf(wt, 512)
            pl.barrier()
            for hd in range(4):
                wcol = [wbf[ch][:, hd * 128:(hd + 1) * 128] for ch in range(NCH)]
                for tt in range(NTT):
                    bank = tt % 2
                    proj_ws(bank, wcol, ybf, tt)
                    pl.op("vector", "tensor_copy",
                          convb[:, 4 + tt * 512:4 + (tt + 1) * 512],
                          pb[bank][:, 0:512],
                          waits=(s_pe.now(),) + (pl.g("convb") if tt == 0 else ()),
                          inc=s_dv)
                    pl.setg(("pb", bank), s_dv.now())
                pl.op("gpsimd", "memset", convb[:, 0:4], 0.0,
                      waits=pl.g("convb"), inc=s_gp)
                pl.add("vector", None, waits=(s_gp.now(),))
                cwcol = [cw[:, hd * 4 + j:hd * 4 + j + 1] for j in range(4)]
                conv_silu(cwcol, qsil[0])
                r1 = tmpA.bitcast(BF16)[:, 0:T]
                for tt in range(NTT):
                    bank = 4 + tt % 2
                    pl.op("tensor", "matmul", pb[bank][:, 0:512], rmat,
                          qsil[0][:, tt * 512:(tt + 1) * 512], start=True, stop=True,
                          waits=(s_ac.now(),) + pl.g(("pb", bank)), inc=s_pe)
                    pl.op("vector", "tensor_mul", r1[:, tt * 512:(tt + 1) * 512],
                          pb[bank][:, 0:512], sinb[:, tt * 512:(tt + 1) * 512],
                          waits=(s_pe.now(),), inc=s_dv)
                    pl.setg(("pb", bank), s_dv.now())
                pl.op("gpsimd", "tensor_mul", qsil[1], qsil[0], cosb,
                      waits=(s_ac.now(),) + pl.g("qsil1"), inc=s_gp)
                if kind == "q":
                    pl.op("vector", "tensor_add", qfm[hd], qsil[1], r1,
                          waits=(s_gp.now(), s_dv.now()), inc=s_dv)
                    pl.setg("qsil1", s_dv.now())
                else:
                    kout = convb[:, 0:T]
                    pl.op("vector", "tensor_add", kout, qsil[1], r1,
                          waits=(s_gp.now(), s_dv.now()), inc=s_dv)
                    pl.setg("qsil1", s_dv.now())
                    pl.dma("gpsimd", dap(dKfm, (b * H + hd) * 128 * T, [[T, 128], [1, T]]),
                           kout, s_st, waits=(s_dv.now(),))
                    pl.setg("convb", s_st.now())
        pl.barrier()

        weight_to_bf(wv_t, D)
        pl.barrier()
        for vc in range(NCH):
            wcol = [wbf[ch][:, vc * 128:(vc + 1) * 128] for ch in range(NCH)]
            for tt in range(NTT):
                bank = tt % 2
                proj_ws(bank, wcol, ybf, tt)
                pl.op("vector", "tensor_copy", convb[:, 4 + tt * 512:4 + (tt + 1) * 512],
                      pb[bank][:, 0:512], waits=(s_pe.now(),), inc=s_dv)
                pl.setg(("pb", bank), s_dv.now())
            pl.op("gpsimd", "memset", convb[:, 0:4], 0.0, inc=s_gp)
            pl.add("vector", None, waits=(s_gp.now(),) + pl.g("vsil"))
            cwcol = [vcw[:, vc * 4 + j:vc * 4 + j + 1] for j in range(4)]
            conv_silu(cwcol, qsil[0])
            pl.dma("gpsimd", dap(dVfm, (b * NCH + vc) * 128 * T, [[T, 128], [1, T]]),
                   qsil[0], s_st, waits=(s_ac.now(),))
            pl.setg("vsil", s_st.now())
        pl.barrier()

        weight_to_bf(ws_t, 256)
        pl.barrier()
        for sc in range(2):
            wcol = [wbf[ch][:, sc * 128:(sc + 1) * 128] for ch in range(NCH)]
            for tt in range(NTT):
                bank = tt % 2
                proj_ws(bank, wcol, ybf, tt, extraw=(pl.g("esm") if tt == 0 else ()))
                pl.op("vector", "tensor_scalar", sfp[:, tt * 512:(tt + 1) * 512],
                      pb[bank][:, 0:512], 32.0, -32.0, ALU.min, ALU.max,
                      waits=(s_pe.now(),), inc=s_dv)
                pl.setg(("pb", bank), s_dv.now())
            pl.op("scalar", "activation", qsil[0], sfp, AF.Exp,
                  waits=(s_dv.now(),), inc=s_ac)
            pl.dma("gpsimd", dap(dEs, (b * 2 + sc) * 128 * T, [[T, 128], [1, T]]),
                   qsil[0], s_st, waits=(s_ac.now(),))
            pl.setg("esm", s_st.now())
            pl.op("vector", "tensor_tensor_scan", tmpA, qsil[0], qsil[0], 0.0,
                  ALU.add, ALU.max, waits=(s_ac.now(),), inc=s_dv)
            pl.op("scalar", "activation", tmpB, tmpA, AF.Ln,
                  waits=(s_dv.now(),), inc=s_ac)
            pl.op("scalar", "activation", tmpA, tmpB, AF.Exp, scale=-1.0, inc=s_ac)
            for hh in range(2):
                h = sc * 2 + hh
                for tb in range(NTB):
                    pl.op("tensor", "transpose", pb[6][0:128, 0:64],
                          tmpA[hh * 64:(hh + 1) * 64, tb * 128:(tb + 1) * 128],
                          i64f[hh * 64:(hh + 1) * 64, :],
                          waits=(s_ac.now(),) + pl.g(("pb", 6)), inc=s_pe)
                    pl.op("vector", "tensor_copy", xabt[:, tb * 64:(tb + 1) * 64],
                          pb[6][0:128, 0:64],
                          waits=(s_pe.now(),) + (pl.g("xabt") if tb == 0 else ()),
                          inc=s_dv)
                    pl.setg(("pb", 6), s_dv.now())
                pl.dma("gpsimd", dap(dXab, (b * H + h) * 128 * NTB * M,
                                     [[NTB * M, 128], [1, NTB * M]]),
                       xabt, s_st, waits=(s_dv.now(),))
                pl.setg("xabt", s_st.now())
        pl.barrier()

        weight_to_bf(wg_t, D)
        pl.barrier()
        for tb in range(NTB):
            for half in range(2):
                bank = half
                for ch in range(NCH):
                    pl.op("tensor", "matmul", pb[bank][:, 0:512],
                          ybf[ch][:, tb * 128:(tb + 1) * 128],
                          wbf[ch][:, half * 512:(half + 1) * 512],
                          start=(ch == 0), stop=(ch == NCH - 1),
                          waits=(pl.g(("pb", bank)) if ch == 0 else ()),
                          inc=s_pe if ch == NCH - 1 else None)
                pl.op("scalar", "activation", qsil[0][:, 0:512], pb[bank][:, 0:512],
                      AF.Silu, waits=(s_pe.now(),) + pl.g("gsilbuf"), inc=s_ac)
                pl.setg(("pb", bank), s_ac.now())
                pl.dma("gpsimd",
                       dap(dGtm, (b * NTB + tb) * 128 * D + half * 512,
                           [[D, 128], [1, 512]]),
                       qsil[0][:, 0:512], s_st, waits=(s_ac.now(),))
                pl.setg("gsilbuf", s_st.now())
        pl.barrier()
        if upto < 3:
            continue

        ser(1)
        for h in range(H):
            sc, hh = h // 2, h % 2
            pl.dma("sync", esm_h,
                   dap(dEs, (b * 2 + sc) * 128 * T + hh * 64 * T, [[T, 64], [1, T]]),
                   s_ld, waits=pl.g("esm_mm"))
            pl.dma("sync", xabt, dap(dXab, (b * H + h) * 128 * NTB * M,
                                     [[NTB * M, 128], [1, NTB * M]]), s_ld,
                   waits=pl.g("xabt_mm"))
            pl.op("gpsimd", "memset", s1f, 0.0, waits=pl.g("smset"), inc=s_gp)
            pl.op("gpsimd", "memset", s1b, 0.0, inc=s_gp)
            pl.op("gpsimd", "memset", s2f, 0.0, inc=s_gp)
            pl.op("gpsimd", "memset", s2b, 0.0, inc=s_gp)
            pl.op("gpsimd", "memset", s1alt, 0.0, inc=s_gp)
            pl.op("gpsimd", "memset", s2alt, 0.0, inc=s_gp)
            WG = (s_gp.now(),)
            for tb in range(NTB):
                i = tb % 2
                pl.dma("sync", eraw[i],
                       dap(dEs, (b * 2 + sc) * 128 * T + hh * 64 * T + tb * 128,
                           [[T, 64], [1, 128]]), s_ld)
                pl.dma("sync", kch[i],
                       dap(dKfm, (b * H + h) * 128 * T + tb * 128, [[T, 128], [1, 128]]),
                       s_ld, waits=pl.g(("kch", i)))
                for vh in range(2):
                    pl.dma("sync", vraw[i][:, vh * 128:(vh + 1) * 128],
                           dap(dVfm, (b * NCH + h * 2 + vh) * 128 * T + tb * 128,
                               [[T, 128], [1, 128]]), s_ld)
                pl.dma("sync", gsil[i],
                       dap(dGtm, (b * NTB + tb) * 128 * D + h * HV, [[D, 128], [1, HV]]),
                       s_ld, waits=pl.g(("gsil", i)))
                pb7b = pb[7].bitcast(BF16)
                pl.op("tensor", "transpose", pb7b[0:128, 0:64], eraw[i],
                      i128b[0:64, 0:64], inc=s_pe)
                pl.op("vector", "tensor_copy", estm[i], pb7b[0:128, 0:64], inc=s_dv)
                pl.op("tensor", "transpose", pb7b[0:128, 0:128], kch[i], i128b,
                      inc=s_pe)
                pl.op("vector", "tensor_copy", ktm[i], pb7b[0:128, 0:128], inc=s_dv)
                for vh in range(2):
                    pl.op("tensor", "transpose", pb7b[0:128, 0:128],
                          vraw[i][:, vh * 128:(vh + 1) * 128], i128b, inc=s_pe)
                    pl.op("vector", "tensor_copy", vtm[i][:, vh * 128:(vh + 1) * 128],
                          pb7b[0:128, 0:128], inc=s_dv)
                WT = ()
                pl.op("tensor", "matmul", pb[0][:, 0:128], kch[i],
                      qfm[h][:, tb * 128:(tb + 1) * 128], start=True, stop=True,
                      waits=WG + (s_ld.now(),) + pl.g(("pb", 0)), inc=s_pe)
                pl.setg(("kch", i), s_pe.now())
                pl.op("vector", "tensor_mul", atm, pb[0][:, 0:128], triq,
                      waits=(s_pe.now(),) + pl.g("atm"), inc=s_dv)
                pl.setg(("pb", 0), s_dv.now())
                pl.op("tensor", "matmul", pb[1][:, 0:64], atm, estm[i],
                      start=True, stop=False,
                      waits=(s_dv.now(),) + WT + pl.g(("pb", 1)))
                pl.op("tensor", "matmul", pb[1][:, 0:64],
                      qfm[h][:, tb * 128:(tb + 1) * 128], s1b,
                      start=False, stop=True, waits=pl.g("s1b"), inc=s_pe)
                pl.setg("atm", s_pe.now())
                pl.op("vector", "tensor_mul", okt, pb[1][:, 0:64],
                      xabt[:, tb * 64:(tb + 1) * 64],
                      waits=(s_pe.now(), s_ld.now()), inc=s_dv)
                pl.setg(("pb", 1), s_dv.now())
                pl.op("scalar", "activation", eok, okt, AF.Exp,
                      waits=(s_dv.now(),), inc=s_ac)
                pl.op("vector", "reduce_sum", sumc, eok, AX.X,
                      waits=(s_ac.now(),), inc=s_dv)
                pl.op("vector", "reciprocal", rsumc, sumc, inc=s_dv)
                pl.op("vector", "scalar_tensor_tensor", qvxa, eok, rsumc,
                      xabt[:, tb * 64:(tb + 1) * 64], ALU.mult, ALU.mult, inc=s_dv)
                pb2b = pb[2].bitcast(BF16)
                pl.op("tensor", "transpose", pb2b[0:64, 0:128], qvxa, i128b,
                      waits=(s_dv.now(),) + pl.g(("pb", 2)), inc=s_pe)
                pl.op("vector", "tensor_copy", qvxat, pb2b[0:64, 0:128],
                      waits=(s_pe.now(),) + pl.g("qvxat"), inc=s_dv)
                pl.setg(("pb", 2), s_dv.now())
                pl.op("tensor", "matmul", pb[3][:, 0:128],
                      esm_h[:, tb * 128:(tb + 1) * 128], qvxat,
                      start=True, stop=True, waits=(s_dv.now(),) + pl.g(("pb", 3)),
                      inc=s_pe)
                pl.op("vector", "tensor_mul", wtm, pb[3][:, 0:128], tri,
                      waits=(s_pe.now(),) + pl.g("wtm"), inc=s_dv)
                pl.setg(("pb", 3), s_dv.now())
                pl.op("tensor", "matmul", pb[4][:, 0:HV], wtm, vtm[i],
                      start=True, stop=False, waits=(s_dv.now(),) + pl.g(("pb", 4)))
                pl.op("tensor", "matmul", pb[4][:, 0:HV], qvxat, s2b,
                      start=False, stop=True, waits=pl.g("s2b"), inc=s_pe)
                pl.setg("wtm", s_pe.now())
                pl.setg("qvxat_mm", s_pe.now())
                pl.op("vector", "tensor_copy", o1t, pb[4][:, 0:HV],
                      waits=(s_pe.now(),) + pl.g("o1t"), inc=s_dv)
                pl.setg(("pb", 4), s_dv.now())
                pl.op("scalar", "activation", tmpB[:, 0:HV], o1t, AF.Square,
                      accum_out=ssqc, waits=(s_dv.now(),), inc=s_ac)
                pl.op("vector", "tensor_scalar", rscc, ssqc, 1.0 / HV, 1e-5,
                      ALU.mult, ALU.add, waits=(s_ac.now(),), inc=s_dv)
                pl.op("scalar", "activation", ssqc, rscc, AF.Ln,
                      waits=(s_dv.now(),), inc=s_ac)
                pl.op("scalar", "activation", rscc, ssqc, AF.Exp, scale=-0.5, inc=s_ac)
                pl.op("vector", "scalar_tensor_tensor", tmpA[:, 0:HV], o1t, rscc, gnw,
                      ALU.mult, ALU.mult, waits=(s_ac.now(),), inc=s_dv)
                pl.setg("o1t", s_dv.now())
                pl.op("vector", "tensor_mul", o2t, tmpA[:, 0:HV], gsil[i],
                      waits=(s_ld.now(),) + pl.g("o2t"), inc=s_dv)
                pl.setg(("gsil", i), s_dv.now())
                pl.dma("gpsimd",
                       dap(dOtm, ((b * H + h) * NTB + tb) * 128 * HV,
                           [[HV, 128], [1, HV]]),
                       o2t, s_st, waits=(s_dv.now(),))
                pl.setg("o2t", s_st.now())
                pl.op("tensor", "matmul", pb[5][:, 0:64], ktm[i], estm[i],
                      start=True, stop=True, waits=pl.g(("pb", 5)), inc=s_pe)
                pl.op("tensor", "matmul", pb[6][0:64, 0:HV], estm[i], vtm[i],
                      start=True, stop=True, waits=pl.g(("pb", 6)), inc=s_pe)
                pl.setg(("estm", i), s_pe.now())
                pl.setg(("ktm", i), s_pe.now())
                pl.setg(("vtm", i), s_pe.now())
                W5 = (s_pe.now(),)
                s1cur, s1new = (s1f, s1alt) if tb % 2 == 0 else (s1alt, s1f)
                s2cur, s2new = (s2f, s2alt) if tb % 2 == 0 else (s2alt, s2f)
                pl.op("vector", "tensor_add", s1new, s1cur, pb[5][:, 0:64],
                      waits=W5, inc=s_dv)
                pl.op("vector", "tensor_scalar_mul", s1b, s1new, SCALE, inc=s_dv)
                pl.setg("s1b", s_dv.now())
                pl.op("vector", "tensor_add", s2new, s2cur, pb[6][0:64, 0:HV], inc=s_dv)
                pl.op("vector", "tensor_copy", s2b, s2new, inc=s_dv)
                pl.setg("s2b", s_dv.now())
                pl.setg(("pb", 5), s_dv.now())
                pl.setg(("pb", 6), s_dv.now())
            pl.setg("esm_mm", s_pe.now())
            pl.setg("xabt_mm", s_dv.now())
            pl.setg("smset", s_dv.now())
        pl.barrier()
        if upto < 4:
            continue

        ser(2)
        weight_to_bf(wo_t, D)
        pl.barrier()
        for oc in range(NCH):
            wcol = [wbf[ch][:, oc * 128:(oc + 1) * 128] for ch in range(NCH)]
            for tt in range(NTT):
                bank = tt % 2
                for ch in range(NCH):
                    oi = ch % 2
                    hh2, vh2 = ch // 2, ch % 2
                    for sub in range(4):
                        tb = tt * 4 + sub
                        pl.dma("sync", otf[oi][:, sub * 128:(sub + 1) * 128],
                               dap(dOtm, ((b * H + hh2) * NTB + tb) * 128 * HV + vh2 * 128,
                                   [[HV, 128], [1, 128]]), s_tl,
                               waits=(pl.g(("otf", oi)) if sub == 0 else ()),
                               transpose=True)
                    pl.op("tensor", "matmul", pb[bank][:, 0:512], wcol[ch], otf[oi],
                          start=(ch == 0), stop=(ch == NCH - 1),
                          waits=(s_tl.now(),) + (pl.g(("pb", bank)) if ch == 0 else ()),
                          inc=s_pe)
                    pl.setg(("otf", oi), s_pe.now())
                pl.dma("sync", xch[tt % 2][:, 0:512], dchunk(x0_t, b, oc, tt), s_ld,
                       waits=pl.g(("x1x", tt % 2)))
                pl.op("vector", "tensor_add", tmpA[:, tt * 512:(tt + 1) * 512],
                      pb[bank][:, 0:512], xch[tt % 2][:, 0:512],
                      waits=(s_pe.now(), s_ld.now()) + pl.g(("x1t", tt)), inc=s_dv)
                pl.setg(("pb", bank), s_dv.now())
                pl.setg(("x1x", tt % 2), s_dv.now())
            pl.dma("gpsimd", dchunk(dX1, b, oc), tmpA, s_st, waits=(s_dv.now(),))
            for tt in range(NTT):
                pl.setg(("x1t", tt), s_st.now())
        pl.barrier()

    def ffn_block(src_t, dst_t):
        for b in range(BPC):
            ser(3)
            ln_stats(src_t, b)
            rows_to_blk(b)
            ln_rowmath(b, "ln")
            pl.barrier()
            ln_apply(src_t, b, ybf)
            pl.barrier()
            for part in range(2):
                for blk4 in range(4):
                    oc = part * 4 + blk4
                    for tt in range(NTT):
                        bank = tt % 2
                        wA = wrt if part == 0 else wit
                        wB = nwit if part == 0 else wrt
                        for c in range(4):
                            pl.op("tensor", "matmul", pb[bank][:, 0:512],
                                  wA[c][:, blk4 * 128:(blk4 + 1) * 128],
                                  ybf[c][:, tt * 512:(tt + 1) * 512],
                                  start=(c == 0), stop=False,
                                  waits=(pl.g(("pb", bank)) if c == 0 else ()))
                        for c in range(4):
                            pl.op("tensor", "matmul", pb[bank][:, 0:512],
                                  wB[c][:, blk4 * 128:(blk4 + 1) * 128],
                                  ybf[4 + c][:, tt * 512:(tt + 1) * 512],
                                  start=False, stop=(c == 3),
                                  inc=s_pe if c == 3 else None)
                        pl.dma("sync", xch[tt % 2][:, 0:512], dchunk(src_t, b, oc, tt),
                               s_ld, waits=pl.g(("fx", tt % 2)))
                        fb = fbr if part == 0 else fbi
                        pl.op("vector", "scalar_tensor_tensor",
                              tmpA[:, tt * 512:(tt + 1) * 512],
                              pb[bank][:, 0:512], fb[:, blk4:blk4 + 1],
                              xch[tt % 2][:, 0:512], ALU.add, ALU.add,
                              waits=(s_pe.now(), s_ld.now()) + pl.g(("ft", tt)),
                              inc=s_dv)
                        pl.setg(("pb", bank), s_dv.now())
                        pl.setg(("fx", tt % 2), s_dv.now())
                    pl.dma("gpsimd", dchunk(dst_t, b, oc), tmpA, s_st,
                           waits=(s_dv.now(),))
                    for tt in range(NTT):
                        pl.setg(("ft", tt), s_st.now())
            pl.barrier()

    if upto >= 5:
        ffn_block(dX1, dX2)

    def proj_phase(wt, src_bufs, b, evac, gcol=None, preload=None):
        weight_to_bf(wt, D, gcol=gcol)
        if preload is not None:
            preload()
        pl.barrier()
        for blk8 in range(NCH):
            wcol = [wbf[ch][:, blk8 * 128:(blk8 + 1) * 128] for ch in range(NCH)]
            for tt in range(NTT):
                bank = tt % 2
                proj_ws(bank, wcol, src_bufs, tt)
                evac(blk8, tt, bank)

    if upto >= 6:
        sub = upto if 60 <= upto < 70 else 99
        for b in range(BPC):
            ser(4)
            ln_stats(dX2, b)
            rows_to_blk(b)
            ln_rowmath(b, "ln3")
            pl.barrier()
            if dbg_reps3 is not None:
                for j, rp in enumerate((repA, repB, repC)):
                    pl.dma("gpsimd", dap(dbg_reps3, (b * 3 + j) * 128 * T,
                                         [[T, 128], [1, T]]), rp, s_st,
                           waits=(s_ld.now(),))
                pl.barrier()
            for ch in range(NCH):
                i = ch % 2
                pl.dma("sync", xch[i], dchunk(dX2, b, ch), s_ld, waits=pl.g(("y3x", i)))
                tgt = tmpA if i == 0 else tmpB
                pl.op("gpsimd", "tensor_mul", tgt, xch[i], repA,
                      waits=(s_ld.now(),) + pl.g(("y3t", i)), inc=s_gp)
                pl.op("vector", "tensor_sub", xch[i], tgt, repB,
                      waits=(s_gp.now(),), inc=s_dv)
                pl.op("vector", "tensor_mul", ybf[ch], xch[i], repC, inc=s_dv)
                pl.dma("gpsimd", dchunk(dY3, b, ch), xch[i], s_st, waits=(s_dv.now(),))
                pl.dma("gpsimd", dap(dY3c, (b * NCH + ch) * 128 * T, [[T, 128], [1, T]]),
                       ybf[ch], s_st)
                pl.setg(("y3x", i), s_st.now())
                pl.setg(("y3t", i), s_st.now())
            pl.barrier()

            if sub < 61:
                continue
            def evac_lin(blk8, tt, bank):
                W = (s_pe.now(),)
                pl.op("vector", "tensor_scalar_add", qsil[0][:, tt * 512:(tt + 1) * 512],
                      pb[bank][:, 0:512], bcols[:, 0 * 8 + blk8:0 * 8 + blk8 + 1],
                      waits=W + (pl.g("linq0") if tt == 0 else ()), inc=s_dv)
                pl.op("scalar", "activation", qsil[1][:, tt * 512:(tt + 1) * 512],
                      pb[bank][:, 0:512], AF.Gelu,
                      bias=bcols[:, 0 * 8 + blk8:0 * 8 + blk8 + 1],
                      waits=W + (s_dv.now(),) + (pl.g("linq1") if tt == 0 else ()),
                      inc=s_ac)
                pl.setg(("pb", bank), s_dv.now(), s_ac.now())
                if tt == NTT - 1:
                    pl.dma("gpsimd",
                           dap(dLin, (b * NCH + blk8) * 128 * T, [[T, 128], [1, T]]),
                           qsil[0], s_st, waits=(s_dv.now(),))
                    pl.dma("gpsimd",
                           dap(dXgel, (b * NCH + blk8) * 128 * T, [[T, 128], [1, T]]),
                           qsil[1], s_st, waits=(s_ac.now(),))
                    pl.setg("linq0", s_st.now())
                    pl.setg("linq1", s_st.now())
            proj_phase(wlin_t, ybf, b, evac_lin, gcol=g1c)
            pl.barrier()

            if sub < 62:
                continue
            def preload_lin():
                for ch in range(NCH):
                    pl.dma("sync", ybf[ch],
                           dap(dLin, (b * NCH + ch) * 128 * T, [[T, 128], [1, T]]), s_ld)

            def evac_conv(blk8, tt, bank):
                pl.op("vector", "tensor_scalar_add", tmpA[:, tt * 512:(tt + 1) * 512],
                      pb[bank][:, 0:512], bcols[:, 1 * 8 + blk8:1 * 8 + blk8 + 1],
                      waits=(s_pe.now(),) + (pl.g("cvt") if tt == 0 else ()), inc=s_dv)
                pl.setg(("pb", bank), s_dv.now())
                if tt == NTT - 1:
                    pl.op("vector", "tensor_copy", qsil[0], tmpA,
                          waits=pl.g("cvq"), inc=s_dv)
                    pl.dma("gpsimd", dchunk(dXrg, b, blk8), tmpA, s_st,
                           waits=(s_dv.now(),))
                    pl.dma("gpsimd",
                           dap(dXrgb, (b * NCH + blk8) * 128 * T, [[T, 128], [1, T]]),
                           qsil[0], s_st)
                    pl.setg("cvt", s_st.now())
                    pl.setg("cvq", s_st.now())
            proj_phase(wconv_t, ybf, b, evac_conv, preload=preload_lin)
            pl.barrier()

            if sub < 63:
                continue
            xrgb_sb = ybf

            def preload_xrgb():
                for ch in range(NCH):
                    pl.dma("sync", xrgb_sb[ch],
                           dap(dXrgb, (b * NCH + ch) * 128 * T, [[T, 128], [1, T]]), s_ld)

            def evac_wa(blk8, tt, bank):
                pl.op("scalar", "activation", tmpA[:, tt * 512:(tt + 1) * 512],
                      pb[bank][:, 0:512], AF.Sigmoid,
                      bias=bcols[:, 3 * 8 + blk8:3 * 8 + blk8 + 1],
                      waits=(s_pe.now(),) + (pl.g("wat") if tt == 0 else ()), inc=s_ac)
                pl.setg(("pb", bank), s_ac.now())
                if tt == NTT - 1:
                    pl.op("vector", "tensor_scalar_mul", tmpB, tmpA,
                          logac[:, blk8:blk8 + 1],
                          waits=(s_ac.now(),) + pl.g("wab"), inc=s_dv)
                    pl.dma("gpsimd", dchunk(dLa, b, blk8), tmpB, s_st,
                           waits=(s_dv.now(),))
                    pl.setg("wat", s_st.now())
                    pl.setg("wab", s_st.now())
            proj_phase(wa_t, xrgb_sb, b, evac_wa, preload=preload_xrgb)
            pl.barrier()

            if sub < 64:
                continue
            def evac_wx(blk8, tt, bank):
                pl.op("scalar", "activation", tmpA[:, tt * 512:(tt + 1) * 512],
                      pb[bank][:, 0:512], AF.Sigmoid,
                      bias=bcols[:, 4 * 8 + blk8:4 * 8 + blk8 + 1],
                      waits=(s_pe.now(),) + (pl.g("wxt") if tt == 0 else ()), inc=s_ac)
                pl.setg(("pb", bank), s_ac.now())
                if tt == NTT - 1:
                    pl.dma("sync", xch[0], dchunk(dLa, b, blk8), s_ld,
                           waits=pl.g("rgx0"))
                    pl.dma("sync", xch[1], dchunk(dXrg, b, blk8), s_ld,
                           waits=pl.g("rgx1"))
                    pl.dma("sync", qsil[0],
                           dap(dXgel, (b * NCH + blk8) * 128 * T, [[T, 128], [1, T]]),
                           s_ld, waits=pl.g("rgq0"))
                    WL = (s_ld.now(), s_ac.now())
                    pl.op("scalar", "activation", sfp, xch[0], AF.Exp, waits=WL, inc=s_ac)
                    pl.setg("rgx0", s_ac.now())
                    pl.op("vector", "tensor_scalar_min", tmpB, sfp, 1.0,
                          waits=(s_ac.now(),) + pl.g("scrfT"), inc=s_dv)
                    pl.op("vector", "tensor_scalar", repA, tmpB, -1.0, 1.0,
                          ALU.mult, ALU.add, waits=pl.g("scrfA"), inc=s_dv)
                    pl.op("vector", "scalar_tensor_tensor", repB, tmpB, 1.0, repA,
                          ALU.add, ALU.mult, waits=pl.g("scrfB"), inc=s_dv)
                    pl.op("scalar", "activation", repA, repB, AF.Sqrt,
                          waits=(s_dv.now(),), inc=s_ac)
                    pl.op("vector", "tensor_mul", repB, tmpA, xch[1],
                          waits=(s_ac.now(),), inc=s_dv)
                    pl.setg("rgx1", s_dv.now())
                    pl.op("vector", "tensor_mul", tmpA, repA, repB,
                          waits=(s_ac.now(),), inc=s_dv)
                    pl.setg("scrfA", s_dv.now())
                    pl.op("vector", "tensor_tensor_scan", repB, tmpB, tmpA, 0.0,
                          ALU.mult, ALU.add, inc=s_dv)
                    pl.setg("wxt", s_dv.now())
                    pl.setg("scrfB", s_dv.now())
                    pl.setg("scrfT", s_dv.now())
                    pl.op("vector", "tensor_mul", qsil[1], qsil[0], repB,
                          waits=(s_ld.now(),) + pl.g("rgq1"), inc=s_dv)
                    pl.setg("rgq0", s_dv.now())
                    pl.dma("gpsimd",
                           dap(dXgh, (b * NCH + blk8) * 128 * T, [[T, 128], [1, T]]),
                           qsil[1], s_st, waits=(s_dv.now(),))
                    pl.setg("rgq1", s_st.now())
            proj_phase(wx_t, xrgb_sb, b, evac_wx)
            pl.barrier()

            if sub < 65:
                continue
            def preload_xgh():
                for ch in range(NCH):
                    pl.dma("sync", xrgb_sb[ch],
                           dap(dXgh, (b * NCH + ch) * 128 * T, [[T, 128], [1, T]]), s_ld)

            def evac_lin2(blk8, tt, bank):
                pl.dma("sync", xch[tt % 2][:, 0:512], dchunk(dY3, b, blk8, tt), s_ld,
                       waits=pl.g(("l2x", tt % 2)))
                pl.op("vector", "scalar_tensor_tensor", tmpA[:, tt * 512:(tt + 1) * 512],
                      pb[bank][:, 0:512], bcols[:, 2 * 8 + blk8:2 * 8 + blk8 + 1],
                      xch[tt % 2][:, 0:512], ALU.add, ALU.add,
                      waits=(s_pe.now(), s_ld.now()) + (pl.g("l2t") if tt == 0 else ()),
                      inc=s_dv)
                pl.setg(("pb", bank), s_dv.now())
                pl.setg(("l2x", tt % 2), s_dv.now())
                if tt == NTT - 1:
                    pl.dma("gpsimd", dchunk(dResB, b, blk8), tmpA, s_st,
                           waits=(s_dv.now(),))
                    pl.setg("l2t", s_st.now())
            proj_phase(wlin2_t, xrgb_sb, b, evac_lin2, preload=preload_xgh)
            pl.barrier()

            if sub < 66:
                continue
            ser(5)
            ln_stats(dResB, b, need=1)
            rows_to_blk(b)
            ln_rowmath(b, "rms")
            pl.barrier()
            for ch in range(NCH):
                i = ch % 2
                pl.dma("sync", xch[i], dchunk(dResB, b, ch), s_ld,
                       waits=pl.g(("xm", i)))
                pl.op("vector", "tensor_mul", ybf[ch], xch[i], repA,
                      waits=(s_ld.now(),), inc=s_dv)
                pl.setg(("xm", i), s_dv.now())
            pl.barrier()

            if sub < 67:
                continue
            def evac_w1(blk8, tt, bank):
                pl.op("scalar", "activation", tmpA[:, tt * 512:(tt + 1) * 512],
                      pb[bank][:, 0:512], AF.Sigmoid,
                      bias=bcols[:, 5 * 8 + blk8:5 * 8 + blk8 + 1],
                      waits=(s_pe.now(),) + (pl.g("w1t") if tt == 0 else ()), inc=s_ac)
                pl.setg(("pb", bank), s_ac.now())
                if tt == NTT - 1:
                    pl.op("scalar", "activation", qsil[0], tmpA, AF.Gelu,
                          waits=pl.g("w1q"), inc=s_ac)
                    pl.setg("w1t", s_ac.now())
                    pl.dma("gpsimd",
                           dap(dGate, (b * NCH + blk8) * 128 * T, [[T, 128], [1, T]]),
                           qsil[0], s_st, waits=(s_ac.now(),))
                    pl.setg("w1q", s_st.now())
            proj_phase(w1_t, ybf, b, evac_w1, gcol=g2c)
            pl.barrier()

            if sub < 68:
                continue
            def evac_w2(blk8, tt, bank):
                pl.op("vector", "tensor_scalar_add", tmpA[:, tt * 512:(tt + 1) * 512],
                      pb[bank][:, 0:512], bcols[:, 6 * 8 + blk8:6 * 8 + blk8 + 1],
                      waits=(s_pe.now(),) + (pl.g("w2t") if tt == 0 else ()), inc=s_dv)
                pl.setg(("pb", bank), s_dv.now())
                if tt == NTT - 1:
                    pl.dma("sync", qsil[0],
                           dap(dGate, (b * NCH + blk8) * 128 * T, [[T, 128], [1, T]]),
                           s_ld, waits=pl.g("w2g"))
                    pl.op("vector", "tensor_mul", qsil[1], tmpA, qsil[0],
                          waits=(s_ld.now(), s_dv.now()) + pl.g("w2q"), inc=s_dv)
                    pl.setg("w2t", s_dv.now())
                    pl.setg("w2g", s_dv.now())
                    pl.dma("gpsimd",
                           dap(dComb, (b * NCH + blk8) * 128 * T, [[T, 128], [1, T]]),
                           qsil[1], s_st, waits=(s_dv.now(),))
                    pl.setg("w2q", s_st.now())
            proj_phase(w2_t, ybf, b, evac_w2, gcol=g2c)
            pl.barrier()

            if sub < 69:
                continue
            def preload_comb():
                for ch in range(NCH):
                    pl.dma("sync", xrgb_sb[ch],
                           dap(dComb, (b * NCH + ch) * 128 * T, [[T, 128], [1, T]]), s_ld)

            def evac_w3(blk8, tt, bank):
                pl.dma("sync", xch[tt % 2][:, 0:512], dchunk(dResB, b, blk8, tt), s_ld,
                       waits=pl.g(("w3x", tt % 2)))
                pl.op("vector", "scalar_tensor_tensor", tmpA[:, tt * 512:(tt + 1) * 512],
                      pb[bank][:, 0:512], bcols[:, 7 * 8 + blk8:7 * 8 + blk8 + 1],
                      xch[tt % 2][:, 0:512], ALU.add, ALU.add,
                      waits=(s_pe.now(), s_ld.now()) + (pl.g("w3t") if tt == 0 else ()),
                      inc=s_dv)
                pl.setg(("pb", bank), s_dv.now())
                pl.dma("sync", xch[tt % 2][:, 0:512], dchunk(dX2, b, blk8, tt), s_ld,
                       waits=(s_dv.now(),))
                pl.op("vector", "tensor_add", tmpB[:, tt * 512:(tt + 1) * 512],
                      tmpA[:, tt * 512:(tt + 1) * 512], xch[tt % 2][:, 0:512],
                      waits=(s_ld.now(),) + (pl.g("w3tb") if tt == 0 else ()), inc=s_dv)
                pl.setg(("w3x", tt % 2), s_dv.now())
                if tt == NTT - 1:
                    pl.dma("gpsimd", dchunk(dX3, b, blk8), tmpB, s_st,
                           waits=(s_dv.now(),))
                    pl.setg("w3t", s_st.now())
                    pl.setg("w3tb", s_st.now())
            proj_phase(w3_t, xrgb_sb, b, evac_w3, preload=preload_comb)
            ser(9)
            pl.barrier()

    if upto >= 7:
        ffn_block(dX3, out_t)

    for tp, (src, dst) in tap_outs.items():
        n = int(np.prod(src.shape))
        step = 128 * T
        for j in range(0, n, step):
            w = min(step, n - j)
            pl.dma("sync", dap(dst, j, [[1, 1], [1, w]]), dap(src, j, [[1, 1], [1, w]]),
                   s_ld)
    pl.barrier(force=True)
    pl.add("sync", None, waits=tuple((s, s.n) for s in pl.dma_sems))

    pl.emit()
    return nc


def _prep_inputs(inputs):
    bf = ml_dtypes.bfloat16
    x = np.ascontiguousarray(np.asarray(inputs["L0_attn_a_hidden_states"], np.float32))
    G1R = np.asarray(inputs["L0_ffn_G1_R"], np.float32)
    G2R = np.asarray(inputs["L0_ffn_G2_R"], np.float32)
    G1I = np.asarray(inputs["L0_ffn_G1_I"], np.float32)
    G2I = np.asarray(inputs["L0_ffn_G2_I"], np.float32)

    def tfac(G1, G2):
        G1v = G1[0]
        G2v = G2[:, :, 0, :]
        t1 = np.empty((4, 512, 512), np.float32)
        t2 = np.empty((4, 512, 512), np.float32)
        for c in range(4):
            A = G1v[:, c, :]
            t1[c] = np.broadcast_to(A.T[:, None, :, None], (32, 16, 32, 16)).reshape(512, 512)
            B = G2v[c]
            t2[c] = np.broadcast_to(B.T[None, :, None, :], (32, 16, 32, 16)).reshape(512, 512)
        return np.ascontiguousarray(t1), np.ascontiguousarray(t2)

    t1r, t2r = tfac(G1R, G2R)
    t1i, t2i = tfac(G1I, G2I)

    tri = np.ascontiguousarray(np.triu(np.ones((128, 128), np.float32)))
    triq = np.ascontiguousarray(tri * np.float32(SCALE))
    i64f = np.ascontiguousarray(np.concatenate([np.eye(64, dtype=np.float32)] * 2, axis=0))
    i128b = np.ascontiguousarray(np.eye(128, dtype=np.float32).astype(bf))
    rmat = np.zeros((128, 128), np.float32)
    for kp in range(64):
        rmat[kp + 64, kp] = -1.0
        rmat[kp, kp + 64] = 1.0
    rmat = np.ascontiguousarray(rmat.astype(bf))
    onesb = np.ascontiguousarray(np.ones((128, 1), np.float32).astype(bf))

    f32 = lambda a: np.ascontiguousarray(np.asarray(a, np.float32))
    bcols = np.stack([f32(inputs[k]).reshape(NCH, 128, 1) for k in (
        "L0_attn_b_b_lin", "L0_attn_b_b_conv", "L0_attn_b_b_lin2", "L0_attn_b_ba",
        "L0_attn_b_bx", "L0_attn_b_b1", "L0_attn_b_b2", "L0_attn_b_b3")])

    shared = {
        "wq": f32(inputs["L0_attn_a_Wq"]), "wk": f32(inputs["L0_attn_a_Wk"]),
        "wv": f32(inputs["L0_attn_a_Wv"]), "ws": f32(inputs["L0_attn_a_Ws"]),
        "wg": f32(inputs["L0_attn_a_Wg"]), "wo": f32(inputs["L0_attn_a_Wo"]),
        "qcw": f32(inputs["L0_attn_a_qconv_w"])[:, 0, :].reshape(4, 128, 4).copy(),
        "kcw": f32(inputs["L0_attn_a_kconv_w"])[:, 0, :].reshape(4, 128, 4).copy(),
        "vcw": f32(inputs["L0_attn_a_vconv_w"])[:, 0, :].reshape(8, 128, 4).copy(),
        "gnw": np.ascontiguousarray(
            np.broadcast_to(f32(inputs["L0_attn_a_gnorm_w"]), (128, 256))),
        "wlin": f32(np.asarray(inputs["L0_attn_b_W_lin"]).T),
        "wconv": f32(np.asarray(inputs["L0_attn_b_W_conv"]).T),
        "wlin2": f32(np.asarray(inputs["L0_attn_b_W_lin2"]).T),
        "wa": f32(np.asarray(inputs["L0_attn_b_Wa"]).T),
        "wx": f32(np.asarray(inputs["L0_attn_b_Wx"]).T),
        "w1": f32(np.asarray(inputs["L0_attn_b_W1"]).T),
        "w2": f32(np.asarray(inputs["L0_attn_b_W2"]).T),
        "w3": f32(np.asarray(inputs["L0_attn_b_W3"]).T),
        "bcols": np.ascontiguousarray(bcols),
        "g1": f32(inputs["L0_attn_b_g1"]).reshape(NCH, 128, 1).copy(),
        "g2": f32(inputs["L0_attn_b_g2"]).reshape(NCH, 128, 1).copy(),
        "lam": f32(inputs["L0_attn_b_Lam"]).reshape(NCH, 128, 1).copy(),
        "t1r": t1r, "t2r": t2r, "t1i": t1i, "t2i": t2i,
        "fbr": f32(inputs["L0_ffn_bias_real"]).reshape(4, 128, 1).copy(),
        "fbi": f32(inputs["L0_ffn_bias_imag"]).reshape(4, 128, 1).copy(),
        "tri": tri, "triq": triq, "i64f": i64f, "i128b": i128b, "rmat": rmat,
        "onesb": onesb,
    }
    per_core = []
    for c in range(NCORE):
        xc = x[c * BPC:(c + 1) * BPC]
        xfm = np.ascontiguousarray(xc.transpose(0, 2, 1)).reshape(BPC, NCH, 128, T)
        m = dict(shared)
        m["x0"] = np.ascontiguousarray(xfm)
        per_core.append(m)
    return per_core


SER_MASK = 0


def kernel(**inputs):
    per_core = _prep_inputs(inputs)
    nc = build(ser_mask=SER_MASK)
    res = run_bass_kernel_spmd(nc, per_core, core_ids=list(range(NCORE)))
    outs = []
    for c in range(NCORE):
        o = np.asarray(res.results[c]["out"])
        o = o.reshape(BPC, D, T).transpose(0, 2, 1)
        outs.append(o)
    return np.ascontiguousarray(np.concatenate(outs, axis=0))


# version 16
# speedup vs baseline: 1.0161x; 1.0161x over previous
import math
import numpy as np
import ml_dtypes
import concourse.bass as bass
import concourse.mybir as mybir
from concourse.bass_utils import run_bass_kernel_spmd

F32 = mybir.dt.float32
BF16 = mybir.dt.bfloat16
I32 = mybir.dt.int32
AF = mybir.ActivationFunctionType
ALU = mybir.AluOpType
AX = mybir.AxisListType

NCORE = 8
BPC = 2
T = 2048
D = 1024
NCH = 8
H = 4
M = 64
HK = 128
HV = 256
NTB = 16
NTT = 4
SCALE = HK ** -0.5
ENGS = ("sync", "tensor", "vector", "scalar", "gpsimd")
TAU = 2.0 * math.pi


def _cw_consts():
    def trunc12(x):
        u = np.float32(x).view(np.uint32)
        u = np.uint32(int(u) & ~0xFFF)
        return float(u.view(np.float32))
    c1 = trunc12(TAU)
    c2 = trunc12(TAU - c1)
    c3 = float(np.float32(TAU - c1 - c2))
    return c1, c2, c3


class Sem:
    __slots__ = ("h", "n")

    def __init__(self, nc, name):
        self.h = nc.alloc_semaphore(name)
        self.n = 0

    def now(self):
        return (self, self.n)


class Plan:

    def __init__(self, nc, serialize=False):
        self.nc = nc
        self.serialize = serialize
        self.q = {e: [] for e in ENGS}
        self.bar = Sem(nc, "bar")
        self._last_out = {}
        self.dma_sems = []
        self.dma_pool = []
        self.retired = set()
        self.eng_sem = {}
        self.all_sems = []
        self.wmark = {}
        self.rmark = {}
        self.hwm = {e: {} for e in ENGS}
        self.guards = {}

    def add(self, eng, call=None, waits=(), inc=None, k=1):
        w = []
        hw = self.hwm[eng]
        own = self.eng_sem.get(eng) if eng != "sync" else None
        for s, v in waits:
            v = int(v)
            if v <= 0 or s is own or s in self.retired:
                continue
            if hw.get(s, 0) >= v:
                continue
            hw[s] = v
            w.append((s.h, v))
        ih = None
        if inc is not None:
            ih = inc.h
            inc.n += k
        self.q[eng].append((tuple(w), call, ih, k))

    def _fix_dma(self, m):
        s, v = m
        if s in self._dma_set:
            return (s, s.n)
        return m

    def _dep_waits(self, ins, outs, eng=None):
        if self.serialize:
            return [s.now() for s in self.all_sems]
        ws = []
        for nm in ins:
            m = self.wmark.get(nm)
            if m is not None:
                ws.append(self._fix_dma(m))
            if nm.startswith("pb"):
                for e2, m2 in self.rmark.get(nm, {}).items():
                    if e2 != eng:
                        ws.append(self._fix_dma(m2))
        for nm in outs:
            m = self.wmark.get(nm)
            if m is not None:
                ws.append(self._fix_dma(m))
            for m in self.rmark.get(nm, {}).values():
                ws.append(self._fix_dma(m))
        return ws

    def _dep_update(self, eng, ins, outs, sem, val):
        for nm in outs:
            self.wmark[nm] = (sem, val)
            self.rmark[nm] = {}
        for nm in ins:
            self.rmark.setdefault(nm, {})[eng] = (sem, val)

    def op(self, eng, name, *args, waits=(), inc=None, k=1, **kwargs):
        outs = set()
        ins = set()
        for j, a in enumerate(args):
            nm = getattr(getattr(a, "tensor", None), "name", None)
            if nm is None:
                continue
            (outs if j == 0 else ins).add(nm)
        for kk, a in kwargs.items():
            nm = getattr(getattr(a, "tensor", None), "name", None)
            if nm is None:
                continue
            (outs if kk == "accum_out" else ins).add(nm)
        if eng in ("vector", "scalar", "gpsimd"):
            hist = self._last_out.setdefault(eng, [])
            if any(ins & h for h in hist):
                self.add(eng, ("drain", (), {}))
                hist.clear()
            hist.append(outs)
            del hist[:-2]
        if inc is None:
            inc = self.eng_sem[eng]
            k = 1
        autow = self._dep_waits(ins, outs, eng)
        self.add(eng, (name, args, kwargs), waits=tuple(waits) + tuple(autow),
                 inc=inc, k=k)
        self._dep_update(eng, ins, outs, inc, inc.n)

    def dma(self, eng, out, in_, sem, waits=(), transpose=False, slow=False):
        kw = {"out": out, "in_": in_}
        if transpose:
            kw["transpose"] = True
        if slow:
            kw["allow_slow_non_contiguous"] = True
        ins = set()
        outs = set()
        onm = getattr(getattr(out, "tensor", None), "name", None)
        if onm:
            outs.add(onm)
        nm = getattr(getattr(in_, "tensor", None), "name", None)
        if nm:
            ins.add(nm)
        if self.dma_pool:
            key = onm or nm or "x"
            sem = self.dma_pool[hash(key) % len(self.dma_pool)]
        autow = self._dep_waits(ins, outs, eng)
        self.add(eng, ("dma_start", (), kw), waits=tuple(waits) + tuple(autow),
                 inc=sem, k=16)
        self._dep_update(eng, ins, outs, sem, sem.n)

    def g(self, key):
        return self.guards.get(key, ())

    def setg(self, key, *gs):
        self.guards[key] = tuple(gs)

    def barrier(self, force=False):
        if not force and getattr(self, "no_barriers", False):
            self.guards.clear()
            return
        dws = tuple(s.now() for s in self.dma_sems)
        for e in ENGS:
            for s, v in dws:
                if v > 0:
                    self.add(e, None, waits=((s, v),))
        tgt = self.bar.n + len(ENGS)
        for e in ENGS:
            self.add(e, ("nop", (), {}), inc=self.bar)
        for e in ENGS:
            self.add(e, None, waits=((self.bar, tgt),))
        allsems = set(self.dma_sems) | set(self.eng_sem.values())
        for m in self.wmark.values():
            allsems.add(m[0])
        for e in ENGS:
            for s in allsems:
                self.hwm[e][s] = max(self.hwm[e].get(s, 0), s.n)
        self.guards.clear()

    def emit(self):
        nc = self.nc
        with nc.Block() as block:
            def mk(eng_name):
                def body(eng):
                    for w, call, ih, k in self.q[eng_name]:
                        for hh, v in w:
                            eng.wait_ge(hh, v)
                        if call is None:
                            continue
                        name, args, kwargs = call
                        ins = getattr(eng, name)(*args, **kwargs)
                        if ih is not None:
                            ins.then_inc(ih, k)
                return body
            block.sync(mk("sync"))
            block.tensor(mk("tensor"))
            block.vector(mk("vector"))
            block.scalar(mk("scalar"))
            block.gpsimd(mk("gpsimd"))


def dap(handle, offset, dims):
    return bass.AP(handle, int(offset), [list(d) for d in dims])


def build(upto=99, taps=(), serialize=False, ser_mask=0):
    nc = bass.Bass("TRN2", target_bir_lowering=False)
    C1, C2, C3 = _cw_consts()

    def din(name, shape, dt=F32):
        return nc.dram_tensor(name, list(shape), dt, kind="ExternalInput")

    x0_t = din("x0", (BPC, NCH, 128, T))
    wq_t = din("wq", (D, 512)); wk_t = din("wk", (D, 512))
    wv_t = din("wv", (D, D)); ws_t = din("ws", (D, 256))
    wg_t = din("wg", (D, D)); wo_t = din("wo", (D, D))
    qcw_t = din("qcw", (4, 128, 4)); kcw_t = din("kcw", (4, 128, 4))
    vcw_t = din("vcw", (8, 128, 4))
    gnw_t = din("gnw", (128, 256))
    wlin_t = din("wlin", (D, D)); wconv_t = din("wconv", (D, D))
    wlin2_t = din("wlin2", (D, D)); wa_t = din("wa", (D, D)); wx_t = din("wx", (D, D))
    w1_t = din("w1", (D, D)); w2_t = din("w2", (D, D)); w3_t = din("w3", (D, D))
    bcols_t = din("bcols", (8, NCH, 128, 1))
    g1_t = din("g1", (NCH, 128, 1)); g2_t = din("g2", (NCH, 128, 1))
    lam_t = din("lam", (NCH, 128, 1))
    t1r_t = din("t1r", (4, 512, 512)); t2r_t = din("t2r", (4, 512, 512))
    t1i_t = din("t1i", (4, 512, 512)); t2i_t = din("t2i", (4, 512, 512))
    fbr_t = din("fbr", (4, 128, 1)); fbi_t = din("fbi", (4, 128, 1))
    tri_t = din("tri", (128, 128))
    triq_t = din("triq", (128, 128))
    i64f_t = din("i64f", (128, 64))
    i128b_t = din("i128b", (128, 128), BF16)
    rmat_t = din("rmat", (128, 128), BF16)
    ones_t = din("onesb", (128, 1), BF16)

    out_t = nc.dram_tensor("out", [BPC, NCH, 128, T], F32, kind="ExternalOutput")

    def dsc(name, shape, dt=F32):
        return nc.dram_tensor(name, list(shape), dt, kind="Internal")

    dX1 = dsc("dX1", (BPC, NCH, 128, T))
    dX2 = dsc("dX2", (BPC, NCH, 128, T))
    dX3 = dsc("dX3", (BPC, NCH, 128, T))
    dY3 = dsc("dY3", (BPC, NCH, 128, T))
    dY3c = dsc("dY3c", (BPC, NCH, 128, T), BF16)
    dVfm = dsc("dVfm", (BPC, NCH, 128, T), BF16)
    dKfm = dsc("dKfm", (BPC, H, 128, T), BF16)
    dEs = dsc("dEs", (BPC, 2, 128, T), BF16)
    dGtm = dsc("dGtm", (BPC, NTB, 128, D), BF16)
    dOtm = dsc("dOtm", (BPC, H, NTB, 128, HV), BF16)
    dLin = dsc("dLin", (BPC, NCH, 128, T), BF16)
    dXgel = dsc("dXgel", (BPC, NCH, 128, T), BF16)
    dXrg = dsc("dXrg", (BPC, NCH, 128, T))
    dXrgb = dsc("dXrgb", (BPC, NCH, 128, T), BF16)
    dLa = dsc("dLa", (BPC, NCH, 128, T))
    dXgh = dsc("dXgh", (BPC, NCH, 128, T), BF16)
    dResB = dsc("dResB", (BPC, NCH, 128, T))
    dGate = dsc("dGate", (BPC, NCH, 128, T), BF16)
    dComb = dsc("dComb", (BPC, NCH, 128, T), BF16)
    dXab = dsc("dXab", (BPC, H, 128, NTB * M))
    dRows = dsc("dRows", (BPC, 8192))
    dRows2 = dsc("dRows2", (BPC, 8192))

    tap_map = {"x1": dX1, "x2": dX2, "x3": dX3, "y3": dY3, "y3c": dY3c, "vfm": dVfm,
               "kfm": dKfm, "es": dEs, "gtm": dGtm, "otm": dOtm, "lin": dLin,
               "xgel": dXgel, "xrg": dXrg, "la": dLa, "xgh": dXgh, "resb": dResB,
               "gate": dGate, "comb": dComb}
    tap_outs = {}
    for tp in taps:
        if tp in ("y1", "reps", "rows", "rows2", "reps3"):
            continue
        srct = tap_map[tp]
        tap_outs[tp] = (srct, nc.dram_tensor(
            "tap_" + tp, list(srct.shape), srct.dtype, kind="ExternalOutput"))
    dbg_y1 = dbg_reps = dbg_rows = None
    if "y1" in taps:
        dbg_y1 = nc.dram_tensor("tap_y1", [BPC, NCH, 128, T], BF16, kind="ExternalOutput")
    if "reps" in taps:
        dbg_reps = nc.dram_tensor("tap_reps", [BPC, 2, 128, T], F32, kind="ExternalOutput")
    if "rows" in taps:
        dbg_rows = nc.dram_tensor("tap_rows", [BPC, 8192], F32, kind="ExternalOutput")
    dbg_reps3 = None
    if "reps3" in taps:
        dbg_reps3 = nc.dram_tensor("tap_reps3", [BPC, 3, 128, T], F32,
                                   kind="ExternalOutput")
    dbg_rows2 = None
    if "rows2" in taps:
        dbg_rows2 = nc.dram_tensor("tap_rows2", [BPC, 3, 8192], F32, kind="ExternalOutput")

    def sb(name, shape, dt=F32):
        return nc.alloc_sbuf_tensor("sb_" + name, list(shape), dt).ap()

    tri = sb("tri", (128, 128))
    triq = sb("triq", (128, 128))
    i64f = sb("i64f", (128, 64))
    i128b = sb("i128b", (128, 128), BF16)
    rmat = sb("rmat", (128, 128), BF16)
    onesb = sb("onesb", (128, 1), BF16)
    gnw = sb("gnw", (128, 256))
    qcw = sb("qcw", (128, 16)); kcw = sb("kcw", (128, 16)); vcw = sb("vcw", (128, 32))
    bcols = sb("bcols", (128, 64))
    g1c = sb("g1c", (128, 8)); g2c = sb("g2c", (128, 8))
    logac = sb("logac", (128, 8))
    fbr = sb("fbr", (128, 4)); fbi = sb("fbi", (128, 4))
    cosb = sb("cosb", (128, T), BF16); sinb = sb("sinb", (128, T), BF16)
    wrt = [sb(f"wrt{c}", (128, 512), BF16) for c in range(4)]
    wit = [sb(f"wit{c}", (128, 512), BF16) for c in range(4)]
    nwit = [sb(f"nwit{c}", (128, 512), BF16) for c in range(4)]

    xch = [sb(f"xch{i}", (128, T)) for i in range(2)]
    tmpA = sb("tmpA", (128, T))
    tmpB = sb("tmpB", (128, T))
    repA = sb("repA", (128, T))
    repB = sb("repB", (128, T))
    ybf = [sb(f"ybf{i}", (128, T), BF16) for i in range(NCH)]
    wbf = [sb(f"wbf{i}", (128, 1024), BF16) for i in range(NCH)]
    wstg = [sb(f"wstg{i}", (128, 512)) for i in range(2)]
    blk = sb("blk", (128, 64))
    blk2 = sb("blk2", (128, 64))
    blk3 = sb("blk3", (128, 64))

    qfm = [sb(f"qfm{h}", (128, T), BF16) for h in range(H)]
    sfp = sb("sfp", (128, T))
    repC = sfp
    esm_h = sb("esm_h", (64, T), BF16)
    xabt = sb("xabt", (128, NTB * M))
    convb = sb("convb", (128, T + 4), BF16)
    qsil = [sb(f"qsil{i}", (128, T), BF16) for i in range(2)]
    xbst = [qsil[0][:, 0:512], qsil[0][:, 512:1024]]
    xsqst = [qsil[1][:, 0:512], qsil[1][:, 512:1024]]
    rb1 = tmpA[0:1, 0:2048]
    rb2 = tmpB[0:1, 0:2048]

    atm = sb("atm", (128, 128), BF16)
    kch = [sb(f"kch{i}", (128, 128), BF16) for i in range(4)]
    wtm = sb("wtm", (128, 128), BF16)
    estm = [sb(f"estm{i}", (128, M), BF16) for i in range(4)]
    ktm = [sb(f"ktm{i}", (128, 128), BF16) for i in range(4)]
    vtm = [sb(f"vtm{i}", (128, HV), BF16) for i in range(4)]
    gsil = [sb(f"gsil{i}", (128, HV), BF16) for i in range(4)]
    okt = sb("okt", (128, M))
    eok = sb("eok", (128, M))
    sumc = sb("sumc", (128, 1))
    rsumc = sb("rsumc", (128, 1))
    qvxa = sb("qvxa", (128, M), BF16)
    qvxat = sb("qvxat", (64, 128), BF16)
    s1f = sb("s1f", (128, M)); s1b = sb("s1b", (128, M), BF16)
    s2f = sb("s2f", (64, HV)); s2b = sb("s2b", (64, HV), BF16)
    s1alt = blk[:, 0:64]
    s2alt = tmpA[0:64, 1024:1280]
    ssqc = sb("ssqc", (128, 1))
    rscc = sb("rscc", (128, 1))
    scr256 = sfp[:, 256:512]
    o1t = sfp[:, 0:256]
    o2t = sb("o2t", (128, HV), BF16)
    otf = [convb[:, 0:512], convb[:, 512:1024]]

    pb = [nc.alloc_psum_tensor(f"pb{i}", [128, 512], F32).ap() for i in range(8)]

    pl = Plan(nc, serialize=serialize)
    pl.ser_base = serialize
    pl.no_barriers = True

    def ser(bit):
        pl.serialize = pl.ser_base or bool(ser_mask & (1 << bit))
    s_ld = Sem(nc, "s_ld")
    s_st = Sem(nc, "s_st")
    s_pe = Sem(nc, "s_pe")
    s_dv = Sem(nc, "s_dv")
    s_ac = Sem(nc, "s_ac")
    s_gp = Sem(nc, "s_gp")
    s_tl = Sem(nc, "s_tl")
    pl.dma_pool = [Sem(nc, f"dp{i}") for i in range(32)]
    pl.dma_sems = list(pl.dma_pool)
    pl.retired = {s_ld, s_st, s_tl}
    pl.eng_sem = {"tensor": s_pe, "vector": s_dv, "scalar": s_ac, "gpsimd": s_gp}
    pl.all_sems = pl.dma_sems + [s_pe, s_dv, s_ac, s_gp]
    pl._dma_set = set(pl.dma_sems)

    def dchunk(t, b, ch, tt=None):
        off = (b * NCH + ch) * 128 * T
        if tt is None:
            return dap(t, off, [[T, 128], [1, T]])
        return dap(t, off + tt * 512, [[T, 128], [1, 512]])

    pl.dma("sync", tri, tri_t.ap(), s_ld)
    pl.dma("sync", triq, triq_t.ap(), s_ld)
    pl.dma("sync", i64f, i64f_t.ap(), s_ld)
    pl.dma("sync", i128b, i128b_t.ap(), s_ld)
    pl.dma("sync", rmat, rmat_t.ap(), s_ld)
    pl.dma("sync", onesb, ones_t.ap(), s_ld)
    pl.dma("sync", gnw, gnw_t.ap(), s_ld)
    pl.dma("sync", qcw, dap(qcw_t, 0, [[4, 128], [512, 4], [1, 4]]), s_ld, slow=True)
    pl.dma("sync", kcw, dap(kcw_t, 0, [[4, 128], [512, 4], [1, 4]]), s_ld, slow=True)
    pl.dma("sync", vcw, dap(vcw_t, 0, [[4, 128], [512, 8], [1, 4]]), s_ld, slow=True)
    pl.dma("sync", bcols, dap(bcols_t, 0, [[1, 128], [1024, 8], [128, 8]]), s_ld, slow=True)
    pl.dma("sync", g1c, dap(g1_t, 0, [[1, 128], [128, 8]]), s_ld, slow=True)
    pl.dma("sync", g2c, dap(g2_t, 0, [[1, 128], [128, 8]]), s_ld, slow=True)
    pl.dma("sync", fbr, dap(fbr_t, 0, [[1, 128], [128, 4]]), s_ld, slow=True)
    pl.dma("sync", fbi, dap(fbi_t, 0, [[1, 128], [128, 4]]), s_ld, slow=True)
    pl.dma("sync", blk[:, 0:8], dap(lam_t, 0, [[1, 128], [128, 8]]), s_ld, slow=True)
    W0 = (s_ld.now(),)

    pl.op("scalar", "activation", blk2[:, 0:8], blk[:, 0:8], AF.Exp, scale=-1.0,
          waits=W0, inc=s_ac)
    pl.op("scalar", "activation", blk3[:, 0:8], blk2[:, 0:8], AF.Ln, bias=1.0, inc=s_ac)
    pl.op("vector", "tensor_scalar_mul", logac[:, 0:8], blk3[:, 0:8], -0.125,
          waits=(s_ac.now(),), inc=s_dv)

    kint = repA.bitcast(I32)
    kflt = repB
    pl.op("gpsimd", "iota", tmpA, [[1, T]], channel_multiplier=0,
          allow_small_or_imprecise_dtypes=True, inc=s_gp)
    pl.op("gpsimd", "iota", blk[0:64, 8:9], [[0, 1]], channel_multiplier=1,
          allow_small_or_imprecise_dtypes=True, inc=s_gp)
    pl.op("gpsimd", "iota", blk[64:128, 8:9], [[0, 1]], channel_multiplier=1,
          allow_small_or_imprecise_dtypes=True, inc=s_gp)
    pl.op("scalar", "activation", blk[:, 9:10], blk[:, 8:9], AF.Exp,
          scale=-math.log(1e4) / 64.0, waits=(s_gp.now(),), inc=s_ac)
    pl.op("vector", "tensor_scalar_mul", tmpB, tmpA, blk[:, 9:10],
          waits=(s_ac.now(), s_gp.now()), inc=s_dv)
    pl.op("vector", "tensor_scalar_mul", tmpA, tmpB, 1.0 / TAU, inc=s_dv)
    pl.op("vector", "tensor_copy", kint, tmpA, inc=s_dv)
    pl.op("vector", "tensor_copy", kflt, kint, inc=s_dv)
    pl.op("vector", "scalar_tensor_tensor", tmpA, kflt, -C1, tmpB, ALU.mult, ALU.add, inc=s_dv)
    pl.op("vector", "scalar_tensor_tensor", tmpB, kflt, -C2, tmpA, ALU.mult, ALU.add, inc=s_dv)
    pl.op("vector", "scalar_tensor_tensor", tmpA, kflt, -C3, tmpB, ALU.mult, ALU.add, inc=s_dv)
    pl.op("vector", "tensor_single_scalar", tmpB, tmpA, math.pi, ALU.is_gt, inc=s_dv)
    pl.op("vector", "scalar_tensor_tensor", kflt, tmpB, -TAU, tmpA, ALU.mult, ALU.add, inc=s_dv)
    pl.op("scalar", "activation", tmpA, kflt, AF.Sin, waits=(s_dv.now(),), inc=s_ac)
    pl.op("vector", "tensor_copy", sinb, tmpA, waits=(s_ac.now(),), inc=s_dv)
    pl.op("vector", "tensor_scalar_add", tmpB, kflt, math.pi / 2.0, inc=s_dv)
    pl.op("vector", "tensor_single_scalar", tmpA, tmpB, math.pi, ALU.is_gt, inc=s_dv)
    pl.op("vector", "scalar_tensor_tensor", kflt, tmpA, -TAU, tmpB, ALU.mult, ALU.add, inc=s_dv)
    pl.op("scalar", "activation", tmpA, kflt, AF.Sin, waits=(s_dv.now(),), inc=s_ac)
    pl.op("vector", "tensor_copy", cosb, tmpA, waits=(s_ac.now(),), inc=s_dv)

    for kind in ("r", "i"):
        t1t, t2t, dst = (t1r_t, t2r_t, wrt) if kind == "r" else (t1i_t, t2i_t, wit)
        for ch4 in range(4):
            prods = [tmpA[:, 0:512], tmpB[:, 0:512], tmpA[:, 512:1024], tmpB[:, 512:1024]]
            for c in range(4):
                i = c % 2
                t1s = xch[i][:, 0:512]
                t2s = wstg[i][:, 0:512]
                off = (c * 512 + ch4 * 128) * 512
                pl.dma("sync", t1s, dap(t1t, off, [[512, 128], [1, 512]]), s_ld,
                       waits=pl.g(("ffnw", i)))
                pl.dma("sync", t2s, dap(t2t, off, [[512, 128], [1, 512]]), s_ld)
                pl.op("vector", "tensor_mul", prods[c], t1s, t2s,
                      waits=(s_ld.now(),), inc=s_dv)
                pl.setg(("ffnw", i), s_dv.now())
            pl.op("vector", "tensor_add", wstg[0][:, 0:512], prods[0], prods[1], inc=s_dv)
            pl.op("vector", "tensor_add", wstg[1][:, 0:512], prods[2], prods[3], inc=s_dv)
            pl.setg(("ffnw", 0), s_dv.now())
            pl.setg(("ffnw", 1), s_dv.now())
            pl.op("vector", "tensor_add", tmpA[:, 1024:1536], wstg[0][:, 0:512],
                  wstg[1][:, 0:512], inc=s_dv)
            pl.op("vector", "tensor_copy", dst[ch4], tmpA[:, 1024:1536], inc=s_dv)
            if kind == "i":
                pl.op("vector", "tensor_scalar_mul", nwit[ch4], tmpA[:, 1024:1536],
                      -1.0, inc=s_dv)
    pl.barrier()

    def ln_stats(src_t, b, need=2):
        for tt in range(NTT):
            for ch in range(NCH):
                i = ch % 2
                pl.dma("sync", xch[i][:, 0:512], dchunk(src_t, b, ch, tt), s_ld,
                       waits=pl.g(("st_x", i)))
                WL = (s_ld.now(),)
                if need == 2:
                    pl.op("vector", "tensor_copy", xbst[i], xch[i][:, 0:512],
                          waits=WL, inc=s_dv)
                pl.op("scalar", "activation", xsqst[i], xch[i][:, 0:512], AF.Square,
                      waits=WL, inc=s_ac)
                pl.setg(("st_x", i), s_dv.now(), s_ac.now())
                if need == 2:
                    pl.op("tensor", "matmul", pb[4][0:1, 0:512], onesb, xbst[i],
                          start=(ch == 0), stop=(ch == NCH - 1),
                          waits=(s_dv.now(),) + (pl.g(("st_pb", 4)) if ch == 0 else ()),
                          inc=s_pe)
                pl.op("tensor", "matmul", pb[5][0:1, 0:512], onesb, xsqst[i],
                      start=(ch == 0), stop=(ch == NCH - 1),
                      waits=(s_ac.now(),) + (pl.g(("st_pb", 5)) if ch == 0 else ()),
                      inc=s_pe)
            W = (s_pe.now(),)
            if need == 2:
                pl.op("vector", "tensor_copy", rb1[:, tt * 512:(tt + 1) * 512],
                      pb[4][0:1, 0:512], waits=W, inc=s_dv)
                pl.setg(("st_pb", 4), s_dv.now())
            pl.op("vector", "tensor_copy", rb2[:, tt * 512:(tt + 1) * 512],
                  pb[5][0:1, 0:512], waits=W, inc=s_dv)
            pl.setg(("st_pb", 5), s_dv.now())

    def rows_to_blk(b):
        pl.dma("gpsimd", dap(dRows, b * 8192, [[1, 1], [1, 2048]]),
               rb1, s_st, waits=(s_dv.now(),))
        pl.dma("gpsimd", dap(dRows, b * 8192 + 2048, [[1, 1], [1, 2048]]),
               rb2, s_st)
        pl.dma("sync", blk[:, 0:16], dap(dRows, b * 8192, [[16, 128], [1, 16]]), s_ld,
               waits=(s_st.now(),))
        pl.dma("sync", blk[:, 16:32], dap(dRows, b * 8192 + 2048, [[16, 128], [1, 16]]),
               s_ld)

    def ln_rowmath(b, kind):
        WL = (s_ld.now(),)
        s1 = blk[:, 0:16]
        s2 = blk[:, 16:32]
        if kind == "rms":
            nr = blk2[:, 0:16]
            pl.op("scalar", "activation", nr, s2, AF.Sqrt, waits=WL, inc=s_ac)
            den = blk3[:, 0:16]
            pl.op("vector", "tensor_scalar_max", den, nr, 1e-12,
                  waits=(s_ac.now(),), inc=s_dv)
            lnd = blk2[:, 16:32]
            pl.op("scalar", "activation", lnd, den, AF.Ln, waits=(s_dv.now(),), inc=s_ac)
            r2 = blk3[:, 16:32]
            pl.op("scalar", "activation", r2, lnd, AF.Exp, scale=-1.0, inc=s_ac)
            pl.op("vector", "tensor_scalar_mul", blk2[:, 32:48], r2, 32.0,
                  waits=(s_ac.now(),), inc=s_dv)
            pl.dma("gpsimd", dap(dRows2, b * 8192, [[16, 128], [1, 16]]),
                   blk2[:, 32:48], s_st, waits=(s_dv.now(),))
            pl.dma("sync", repA, dap(dRows2, b * 8192, [[0, 128], [1, 2048]]),
                   s_ld, waits=(s_st.now(),))
            return
        mu = blk3[:, 0:16]
        pl.op("vector", "tensor_scalar_mul", mu, s1, 1.0 / D, waits=WL, inc=s_dv)
        musq = blk2[:, 16:32]
        pl.op("scalar", "activation", musq, mu, AF.Square,
              waits=(s_dv.now(),), inc=s_ac)
        var = blk3[:, 16:32]
        pl.op("vector", "scalar_tensor_tensor", var, s2, 1.0 / D, musq,
              ALU.mult, ALU.subtract, waits=(s_ac.now(),), inc=s_dv)
        varep = blk2[:, 0:16]
        pl.op("vector", "tensor_scalar_add", varep, var, 1e-5, inc=s_dv)
        lnv = blk[:, 32:48]
        pl.op("scalar", "activation", lnv, varep, AF.Ln, waits=(s_dv.now(),), inc=s_ac)
        rstd = blk3[:, 32:48]
        pl.op("scalar", "activation", rstd, lnv, AF.Exp, scale=-0.5, inc=s_ac)
        murs = blk2[:, 32:48]
        pl.op("vector", "tensor_mul", murs, mu, rstd, waits=(s_ac.now(),), inc=s_dv)
        pl.dma("gpsimd", dap(dRows2, b * 8192, [[16, 128], [1, 16]]),
               rstd, s_st, waits=(s_dv.now(),))
        pl.dma("gpsimd", dap(dRows2, b * 8192 + 2048, [[16, 128], [1, 16]]),
               murs, s_st)
        if kind == "ln3":
            sq = blk[:, 48:64]
            pl.op("scalar", "activation", sq, var, AF.Sqrt, scale=float(D), inc=s_ac)
            den = blk2[:, 48:64]
            pl.op("vector", "tensor_mul", den, sq, rstd,
                  waits=(s_ac.now(), s_st.now()), inc=s_dv)
            den2 = blk3[:, 48:64]
            pl.op("vector", "tensor_scalar_max", den2, den, 1e-12, inc=s_dv)
            lnd = blk[:, 0:16]
            pl.op("scalar", "activation", lnd, den2, AF.Ln, waits=(s_dv.now(),), inc=s_ac)
            inv = blk2[:, 0:16]
            pl.op("scalar", "activation", inv, lnd, AF.Exp, scale=-1.0, inc=s_ac)
            rat = blk3[:, 0:16]
            pl.op("vector", "tensor_scalar_mul", rat, inv, 32.0,
                  waits=(s_ac.now(),), inc=s_dv)
            pl.dma("gpsimd", dap(dRows2, b * 8192 + 4096, [[16, 128], [1, 16]]),
                   rat, s_st, waits=(s_dv.now(),))
        WS = (s_st.now(),)
        pl.dma("sync", repA, dap(dRows2, b * 8192, [[0, 128], [1, 2048]]), s_ld, waits=WS)
        pl.dma("sync", repB, dap(dRows2, b * 8192 + 2048, [[0, 128], [1, 2048]]), s_ld)
        if kind == "ln3":
            pl.dma("sync", repC, dap(dRows2, b * 8192 + 4096, [[0, 128], [1, 2048]]), s_ld)

    def weight_to_bf(dram_t, n_out, gcol=None, neg=False):
        for ch in range(NCH):
            for half in range(0, n_out, 512):
                w = min(512, n_out - half)
                i = (ch * 2 + half // 512) % 2
                pl.dma("sync", wstg[i][:, 0:w],
                       dap(dram_t, ch * 128 * n_out + half, [[n_out, 128], [1, w]]),
                       s_ld, waits=pl.g(("wstg", i)))
                WL = (s_ld.now(),)
                if gcol is not None:
                    pl.op("vector", "tensor_scalar_mul", wbf[ch][:, half:half + w],
                          wstg[i][:, 0:w], gcol[:, ch:ch + 1], waits=WL, inc=s_dv)
                elif neg:
                    pl.op("vector", "tensor_scalar_mul", wbf[ch][:, half:half + w],
                          wstg[i][:, 0:w], -1.0, waits=WL, inc=s_dv)
                else:
                    pl.op("vector", "tensor_copy", wbf[ch][:, half:half + w],
                          wstg[i][:, 0:w], waits=WL, inc=s_dv)
                pl.setg(("wstg", i), s_dv.now())

    def ln_apply(src_t, b, dst_list):
        for ch in range(NCH):
            i = ch % 2
            pl.dma("sync", xch[i], dchunk(src_t, b, ch), s_ld, waits=pl.g(("la_x", i)))
            tgt = tmpA if i == 0 else tmpB
            pl.op("gpsimd", "tensor_mul", tgt, xch[i], repA,
                  waits=(s_ld.now(),) + pl.g(("la_t", i)), inc=s_gp)
            pl.setg(("la_x", i), s_gp.now())
            pl.op("vector", "tensor_sub", dst_list[ch], tgt, repB,
                  waits=(s_gp.now(),), inc=s_dv)
            pl.setg(("la_t", i), s_dv.now())

    def proj_ws(bank, wcol, rhs, tt, n_k=NCH, extraw=()):
        for ch in range(n_k):
            w = extraw + (pl.g(("pb", bank)) if ch == 0 else ())
            pl.op("tensor", "matmul", pb[bank][:, 0:512], wcol[ch],
                  rhs[ch][:, tt * 512:(tt + 1) * 512],
                  start=(ch == 0), stop=(ch == n_k - 1), waits=w,
                  inc=s_pe if ch == n_k - 1 else None)

    def conv_silu(cwcol, dst_bf):
        pl.op("vector", "tensor_scalar_mul", tmpA, convb[:, 4:4 + T], cwcol[3],
              inc=s_dv)
        srcs = [convb[:, 3:3 + T], convb[:, 2:2 + T], convb[:, 1:1 + T]]
        bufs = [(tmpA, tmpB), (tmpB, tmpA), (tmpA, tmpB)]
        for j in range(3):
            a, d = bufs[j]
            pl.op("vector", "scalar_tensor_tensor", d, srcs[j], cwcol[2 - j], a,
                  ALU.mult, ALU.add, inc=s_dv)
        pl.op("scalar", "activation", dst_bf, tmpB, AF.Silu,
              waits=(s_dv.now(),), inc=s_ac)

    for b in range(BPC):
        if upto < 1:
            break
        ser(0)
        ln_stats(x0_t, b)
        rows_to_blk(b)
        ln_rowmath(b, "ln")
        pl.barrier()
        if upto < 2:
            continue

        ln_apply(x0_t, b, ybf)
        pl.barrier()
        if dbg_reps is not None:
            pl.dma("gpsimd", dap(dbg_reps, (b * 2 + 0) * 128 * T, [[T, 128], [1, T]]),
                   repA, s_st)
            pl.dma("gpsimd", dap(dbg_reps, (b * 2 + 1) * 128 * T, [[T, 128], [1, T]]),
                   repB, s_st)
        if dbg_rows is not None:
            pass
        if dbg_rows2 is not None:
            pl.dma("gpsimd", dap(dbg_rows2, b * 3 * 8192, [[64, 128], [1, 64]]),
                   blk[:, 0:64], s_st)
            pl.dma("gpsimd", dap(dbg_rows2, b * 3 * 8192 + 8192, [[64, 128], [1, 64]]),
                   blk2[:, 0:64], s_st)
            pl.dma("gpsimd", dap(dbg_rows2, b * 3 * 8192 + 16384, [[64, 128], [1, 64]]),
                   blk3[:, 0:64], s_st)
        if dbg_y1 is not None:
            for ch in range(NCH):
                pl.dma("gpsimd", dap(dbg_y1, (b * NCH + ch) * 128 * T, [[T, 128], [1, T]]),
                       ybf[ch], s_st)
        if dbg_y1 is not None or dbg_reps is not None or dbg_rows is not None:
            pl.barrier()

        for kind in ("q", "k"):
            wt = wq_t if kind == "q" else wk_t
            cw = qcw if kind == "q" else kcw
            weight_to_bf(wt, 512)
            pl.barrier()
            for hd in range(4):
                wcol = [wbf[ch][:, hd * 128:(hd + 1) * 128] for ch in range(NCH)]
                for tt in range(NTT):
                    bank = tt % 2
                    proj_ws(bank, wcol, ybf, tt)
                    pl.op("vector", "tensor_copy",
                          convb[:, 4 + tt * 512:4 + (tt + 1) * 512],
                          pb[bank][:, 0:512],
                          waits=(s_pe.now(),) + (pl.g("convb") if tt == 0 else ()),
                          inc=s_dv)
                    pl.setg(("pb", bank), s_dv.now())
                pl.op("gpsimd", "memset", convb[:, 0:4], 0.0,
                      waits=pl.g("convb"), inc=s_gp)
                pl.add("vector", None, waits=(s_gp.now(),))
                cwcol = [cw[:, hd * 4 + j:hd * 4 + j + 1] for j in range(4)]
                conv_silu(cwcol, qsil[0])
                r1 = tmpA.bitcast(BF16)[:, 0:T]
                for tt in range(NTT):
                    bank = 4 + tt % 2
                    pl.op("tensor", "matmul", pb[bank][:, 0:512], rmat,
                          qsil[0][:, tt * 512:(tt + 1) * 512], start=True, stop=True,
                          waits=(s_ac.now(),) + pl.g(("pb", bank)), inc=s_pe)
                    pl.op("vector", "tensor_mul", r1[:, tt * 512:(tt + 1) * 512],
                          pb[bank][:, 0:512], sinb[:, tt * 512:(tt + 1) * 512],
                          waits=(s_pe.now(),), inc=s_dv)
                    pl.setg(("pb", bank), s_dv.now())
                pl.op("gpsimd", "tensor_mul", qsil[1], qsil[0], cosb,
                      waits=(s_ac.now(),) + pl.g("qsil1"), inc=s_gp)
                if kind == "q":
                    pl.op("vector", "tensor_add", qfm[hd], qsil[1], r1,
                          waits=(s_gp.now(), s_dv.now()), inc=s_dv)
                    pl.setg("qsil1", s_dv.now())
                else:
                    kout = convb[:, 0:T]
                    pl.op("vector", "tensor_add", kout, qsil[1], r1,
                          waits=(s_gp.now(), s_dv.now()), inc=s_dv)
                    pl.setg("qsil1", s_dv.now())
                    pl.dma("gpsimd", dap(dKfm, (b * H + hd) * 128 * T, [[T, 128], [1, T]]),
                           kout, s_st, waits=(s_dv.now(),))
                    pl.setg("convb", s_st.now())
        pl.barrier()

        weight_to_bf(wv_t, D)
        pl.barrier()
        for vc in range(NCH):
            wcol = [wbf[ch][:, vc * 128:(vc + 1) * 128] for ch in range(NCH)]
            for tt in range(NTT):
                bank = tt % 2
                proj_ws(bank, wcol, ybf, tt)
                pl.op("vector", "tensor_copy", convb[:, 4 + tt * 512:4 + (tt + 1) * 512],
                      pb[bank][:, 0:512], waits=(s_pe.now(),), inc=s_dv)
                pl.setg(("pb", bank), s_dv.now())
            pl.op("gpsimd", "memset", convb[:, 0:4], 0.0, inc=s_gp)
            pl.add("vector", None, waits=(s_gp.now(),) + pl.g("vsil"))
            cwcol = [vcw[:, vc * 4 + j:vc * 4 + j + 1] for j in range(4)]
            conv_silu(cwcol, qsil[0])
            pl.dma("gpsimd", dap(dVfm, (b * NCH + vc) * 128 * T, [[T, 128], [1, T]]),
                   qsil[0], s_st, waits=(s_ac.now(),))
            pl.setg("vsil", s_st.now())
        pl.barrier()

        weight_to_bf(ws_t, 256)
        pl.barrier()
        for sc in range(2):
            wcol = [wbf[ch][:, sc * 128:(sc + 1) * 128] for ch in range(NCH)]
            for tt in range(NTT):
                bank = tt % 2
                proj_ws(bank, wcol, ybf, tt, extraw=(pl.g("esm") if tt == 0 else ()))
                pl.op("vector", "tensor_scalar", sfp[:, tt * 512:(tt + 1) * 512],
                      pb[bank][:, 0:512], 32.0, -32.0, ALU.min, ALU.max,
                      waits=(s_pe.now(),), inc=s_dv)
                pl.setg(("pb", bank), s_dv.now())
            pl.op("scalar", "activation", qsil[0], sfp, AF.Exp,
                  waits=(s_dv.now(),), inc=s_ac)
            pl.dma("gpsimd", dap(dEs, (b * 2 + sc) * 128 * T, [[T, 128], [1, T]]),
                   qsil[0], s_st, waits=(s_ac.now(),))
            pl.setg("esm", s_st.now())
            pl.op("vector", "tensor_tensor_scan", tmpA, qsil[0], qsil[0], 0.0,
                  ALU.add, ALU.max, waits=(s_ac.now(),), inc=s_dv)
            pl.op("scalar", "activation", tmpB, tmpA, AF.Ln,
                  waits=(s_dv.now(),), inc=s_ac)
            pl.op("scalar", "activation", tmpA, tmpB, AF.Exp, scale=-1.0, inc=s_ac)
            for hh in range(2):
                h = sc * 2 + hh
                for tb in range(NTB):
                    pl.op("tensor", "transpose", pb[6][0:128, 0:64],
                          tmpA[hh * 64:(hh + 1) * 64, tb * 128:(tb + 1) * 128],
                          i64f[hh * 64:(hh + 1) * 64, :],
                          waits=(s_ac.now(),) + pl.g(("pb", 6)), inc=s_pe)
                    pl.op("vector", "tensor_copy", xabt[:, tb * 64:(tb + 1) * 64],
                          pb[6][0:128, 0:64],
                          waits=(s_pe.now(),) + (pl.g("xabt") if tb == 0 else ()),
                          inc=s_dv)
                    pl.setg(("pb", 6), s_dv.now())
                pl.dma("gpsimd", dap(dXab, (b * H + h) * 128 * NTB * M,
                                     [[NTB * M, 128], [1, NTB * M]]),
                       xabt, s_st, waits=(s_dv.now(),))
                pl.setg("xabt", s_st.now())
        pl.barrier()

        weight_to_bf(wg_t, D)
        pl.barrier()
        for tb in range(NTB):
            for half in range(2):
                bank = half
                for ch in range(NCH):
                    pl.op("tensor", "matmul", pb[bank][:, 0:512],
                          ybf[ch][:, tb * 128:(tb + 1) * 128],
                          wbf[ch][:, half * 512:(half + 1) * 512],
                          start=(ch == 0), stop=(ch == NCH - 1),
                          waits=(pl.g(("pb", bank)) if ch == 0 else ()),
                          inc=s_pe if ch == NCH - 1 else None)
                pl.op("scalar", "activation", qsil[0][:, 0:512], pb[bank][:, 0:512],
                      AF.Silu, waits=(s_pe.now(),) + pl.g("gsilbuf"), inc=s_ac)
                pl.setg(("pb", bank), s_ac.now())
                pl.dma("gpsimd",
                       dap(dGtm, (b * NTB + tb) * 128 * D + half * 512,
                           [[D, 128], [1, 512]]),
                       qsil[0][:, 0:512], s_st, waits=(s_ac.now(),))
                pl.setg("gsilbuf", s_st.now())
        pl.barrier()
        if upto < 3:
            continue

        ser(1)
        for h in range(H):
            sc, hh = h // 2, h % 2
            pl.dma("sync", esm_h,
                   dap(dEs, (b * 2 + sc) * 128 * T + hh * 64 * T, [[T, 64], [1, T]]),
                   s_ld, waits=pl.g("esm_mm"))
            pl.dma("sync", xabt, dap(dXab, (b * H + h) * 128 * NTB * M,
                                     [[NTB * M, 128], [1, NTB * M]]), s_ld,
                   waits=pl.g("xabt_mm"))
            pl.op("gpsimd", "memset", s1f, 0.0, waits=pl.g("smset"), inc=s_gp)
            pl.op("gpsimd", "memset", s1b, 0.0, inc=s_gp)
            pl.op("gpsimd", "memset", s2f, 0.0, inc=s_gp)
            pl.op("gpsimd", "memset", s2b, 0.0, inc=s_gp)
            pl.op("gpsimd", "memset", s1alt, 0.0, inc=s_gp)
            pl.op("gpsimd", "memset", s2alt, 0.0, inc=s_gp)
            WG = (s_gp.now(),)
            for tb in range(NTB):
                i = tb % 4
                pl.dma("sync", estm[i],
                       dap(dEs, (b * 2 + sc) * 128 * T + hh * 64 * T + tb * 128,
                           [[T, 64], [1, 128]]), s_tl,
                       waits=pl.g(("estm", i)), transpose=True)
                pl.dma("sync", ktm[i],
                       dap(dKfm, (b * H + h) * 128 * T + tb * 128, [[T, 128], [1, 128]]),
                       s_tl, waits=pl.g(("ktm", i)), transpose=True)
                for vh in range(2):
                    pl.dma("sync", vtm[i][:, vh * 128:(vh + 1) * 128],
                           dap(dVfm, (b * NCH + h * 2 + vh) * 128 * T + tb * 128,
                               [[T, 128], [1, 128]]), s_tl,
                           waits=(pl.g(("vtm", i)) if vh == 0 else ()), transpose=True)
                pl.dma("sync", gsil[i],
                       dap(dGtm, (b * NTB + tb) * 128 * D + h * HV, [[D, 128], [1, HV]]),
                       s_ld, waits=pl.g(("gsil", i)))
                pl.dma("sync", kch[i],
                       dap(dKfm, (b * H + h) * 128 * T + tb * 128, [[T, 128], [1, 128]]),
                       s_ld, waits=pl.g(("kch", i)))
                WT = (s_tl.now(),)
                pl.op("tensor", "matmul", pb[0][:, 0:128], kch[i],
                      qfm[h][:, tb * 128:(tb + 1) * 128], start=True, stop=True,
                      waits=WG + (s_ld.now(),) + pl.g(("pb", 0)), inc=s_pe)
                pl.setg(("kch", i), s_pe.now())
                pl.op("vector", "tensor_mul", atm, pb[0][:, 0:128], triq,
                      waits=(s_pe.now(),) + pl.g("atm"), inc=s_dv)
                pl.setg(("pb", 0), s_dv.now())
                pl.op("tensor", "matmul", pb[1][:, 0:64], atm, estm[i],
                      start=True, stop=False,
                      waits=(s_dv.now(),) + WT + pl.g(("pb", 1)))
                pl.op("tensor", "matmul", pb[1][:, 0:64],
                      qfm[h][:, tb * 128:(tb + 1) * 128], s1b,
                      start=False, stop=True, waits=pl.g("s1b"), inc=s_pe)
                pl.setg("atm", s_pe.now())
                pl.op("vector", "tensor_mul", okt, pb[1][:, 0:64],
                      xabt[:, tb * 64:(tb + 1) * 64],
                      waits=(s_pe.now(), s_ld.now()), inc=s_dv)
                pl.setg(("pb", 1), s_dv.now())
                pl.op("scalar", "activation", eok, okt, AF.Exp,
                      waits=(s_dv.now(),), inc=s_ac)
                pl.op("vector", "reduce_sum", sumc, eok, AX.X,
                      waits=(s_ac.now(),), inc=s_dv)
                pl.op("vector", "reciprocal", rsumc, sumc, inc=s_dv)
                pl.op("vector", "scalar_tensor_tensor", qvxa, eok, rsumc,
                      xabt[:, tb * 64:(tb + 1) * 64], ALU.mult, ALU.mult, inc=s_dv)
                pb2b = pb[2].bitcast(BF16)
                pl.op("tensor", "transpose", pb2b[0:64, 0:128], qvxa, i128b,
                      waits=(s_dv.now(),) + pl.g(("pb", 2)), inc=s_pe)
                pl.op("vector", "tensor_copy", qvxat, pb2b[0:64, 0:128],
                      waits=(s_pe.now(),) + pl.g("qvxat"), inc=s_dv)
                pl.setg(("pb", 2), s_dv.now())
                pl.op("tensor", "matmul", pb[3][:, 0:128],
                      esm_h[:, tb * 128:(tb + 1) * 128], qvxat,
                      start=True, stop=True, waits=(s_dv.now(),) + pl.g(("pb", 3)),
                      inc=s_pe)
                pl.op("vector", "tensor_mul", wtm, pb[3][:, 0:128], tri,
                      waits=(s_pe.now(),) + pl.g("wtm"), inc=s_dv)
                pl.setg(("pb", 3), s_dv.now())
                pl.op("tensor", "matmul", pb[4][:, 0:HV], wtm, vtm[i],
                      start=True, stop=False, waits=(s_dv.now(),) + pl.g(("pb", 4)))
                pl.op("tensor", "matmul", pb[4][:, 0:HV], qvxat, s2b,
                      start=False, stop=True, waits=pl.g("s2b"), inc=s_pe)
                pl.setg("wtm", s_pe.now())
                pl.setg("qvxat_mm", s_pe.now())
                pl.op("vector", "tensor_copy", o1t, pb[4][:, 0:HV],
                      waits=(s_pe.now(),) + pl.g("o1t"), inc=s_dv)
                pl.setg(("pb", 4), s_dv.now())
                pl.op("scalar", "activation", tmpB[:, 0:HV], o1t, AF.Square,
                      accum_out=ssqc, waits=(s_dv.now(),), inc=s_ac)
                pl.op("vector", "tensor_scalar", rscc, ssqc, 1.0 / HV, 1e-5,
                      ALU.mult, ALU.add, waits=(s_ac.now(),), inc=s_dv)
                pl.op("scalar", "activation", ssqc, rscc, AF.Ln,
                      waits=(s_dv.now(),), inc=s_ac)
                pl.op("scalar", "activation", rscc, ssqc, AF.Exp, scale=-0.5, inc=s_ac)
                pl.op("vector", "scalar_tensor_tensor", tmpA[:, 0:HV], o1t, rscc, gnw,
                      ALU.mult, ALU.mult, waits=(s_ac.now(),), inc=s_dv)
                pl.setg("o1t", s_dv.now())
                pl.op("vector", "tensor_mul", o2t, tmpA[:, 0:HV], gsil[i],
                      waits=(s_ld.now(),) + pl.g("o2t"), inc=s_dv)
                pl.setg(("gsil", i), s_dv.now())
                pl.dma("gpsimd",
                       dap(dOtm, ((b * H + h) * NTB + tb) * 128 * HV,
                           [[HV, 128], [1, HV]]),
                       o2t, s_st, waits=(s_dv.now(),))
                pl.setg("o2t", s_st.now())
                pl.op("tensor", "matmul", pb[5][:, 0:64], ktm[i], estm[i],
                      start=True, stop=True, waits=pl.g(("pb", 5)), inc=s_pe)
                pl.op("tensor", "matmul", pb[6][0:64, 0:HV], estm[i], vtm[i],
                      start=True, stop=True, waits=pl.g(("pb", 6)), inc=s_pe)
                pl.setg(("estm", i), s_pe.now())
                pl.setg(("ktm", i), s_pe.now())
                pl.setg(("vtm", i), s_pe.now())
                W5 = (s_pe.now(),)
                s1cur, s1new = (s1f, s1alt) if tb % 2 == 0 else (s1alt, s1f)
                s2cur, s2new = (s2f, s2alt) if tb % 2 == 0 else (s2alt, s2f)
                pl.op("vector", "tensor_add", s1new, s1cur, pb[5][:, 0:64],
                      waits=W5, inc=s_dv)
                pl.op("vector", "tensor_scalar_mul", s1b, s1new, SCALE, inc=s_dv)
                pl.setg("s1b", s_dv.now())
                pl.op("vector", "tensor_add", s2new, s2cur, pb[6][0:64, 0:HV], inc=s_dv)
                pl.op("vector", "tensor_copy", s2b, s2new, inc=s_dv)
                pl.setg("s2b", s_dv.now())
                pl.setg(("pb", 5), s_dv.now())
                pl.setg(("pb", 6), s_dv.now())
            pl.setg("esm_mm", s_pe.now())
            pl.setg("xabt_mm", s_dv.now())
            pl.setg("smset", s_dv.now())
        pl.barrier()
        if upto < 4:
            continue

        ser(2)
        weight_to_bf(wo_t, D)
        pl.barrier()
        for oc in range(NCH):
            wcol = [wbf[ch][:, oc * 128:(oc + 1) * 128] for ch in range(NCH)]
            for tt in range(NTT):
                bank = tt % 2
                for ch in range(NCH):
                    oi = ch % 2
                    hh2, vh2 = ch // 2, ch % 2
                    for sub in range(4):
                        tb = tt * 4 + sub
                        pl.dma("sync", otf[oi][:, sub * 128:(sub + 1) * 128],
                               dap(dOtm, ((b * H + hh2) * NTB + tb) * 128 * HV + vh2 * 128,
                                   [[HV, 128], [1, 128]]), s_tl,
                               waits=(pl.g(("otf", oi)) if sub == 0 else ()),
                               transpose=True)
                    pl.op("tensor", "matmul", pb[bank][:, 0:512], wcol[ch], otf[oi],
                          start=(ch == 0), stop=(ch == NCH - 1),
                          waits=(s_tl.now(),) + (pl.g(("pb", bank)) if ch == 0 else ()),
                          inc=s_pe)
                    pl.setg(("otf", oi), s_pe.now())
                pl.dma("sync", xch[tt % 2][:, 0:512], dchunk(x0_t, b, oc, tt), s_ld,
                       waits=pl.g(("x1x", tt % 2)))
                pl.op("vector", "tensor_add", tmpA[:, tt * 512:(tt + 1) * 512],
                      pb[bank][:, 0:512], xch[tt % 2][:, 0:512],
                      waits=(s_pe.now(), s_ld.now()) + pl.g(("x1t", tt)), inc=s_dv)
                pl.setg(("pb", bank), s_dv.now())
                pl.setg(("x1x", tt % 2), s_dv.now())
            pl.dma("gpsimd", dchunk(dX1, b, oc), tmpA, s_st, waits=(s_dv.now(),))
            for tt in range(NTT):
                pl.setg(("x1t", tt), s_st.now())
        pl.barrier()

    def ffn_block(src_t, dst_t):
        for b in range(BPC):
            ser(3)
            ln_stats(src_t, b)
            rows_to_blk(b)
            ln_rowmath(b, "ln")
            pl.barrier()
            ln_apply(src_t, b, ybf)
            pl.barrier()
            for part in range(2):
                for blk4 in range(4):
                    oc = part * 4 + blk4
                    for tt in range(NTT):
                        bank = tt % 2
                        wA = wrt if part == 0 else wit
                        wB = nwit if part == 0 else wrt
                        for c in range(4):
                            pl.op("tensor", "matmul", pb[bank][:, 0:512],
                                  wA[c][:, blk4 * 128:(blk4 + 1) * 128],
                                  ybf[c][:, tt * 512:(tt + 1) * 512],
                                  start=(c == 0), stop=False,
                                  waits=(pl.g(("pb", bank)) if c == 0 else ()))
                        for c in range(4):
                            pl.op("tensor", "matmul", pb[bank][:, 0:512],
                                  wB[c][:, blk4 * 128:(blk4 + 1) * 128],
                                  ybf[4 + c][:, tt * 512:(tt + 1) * 512],
                                  start=False, stop=(c == 3),
                                  inc=s_pe if c == 3 else None)
                        pl.dma("sync", xch[tt % 2][:, 0:512], dchunk(src_t, b, oc, tt),
                               s_ld, waits=pl.g(("fx", tt % 2)))
                        fb = fbr if part == 0 else fbi
                        pl.op("vector", "scalar_tensor_tensor",
                              tmpA[:, tt * 512:(tt + 1) * 512],
                              pb[bank][:, 0:512], fb[:, blk4:blk4 + 1],
                              xch[tt % 2][:, 0:512], ALU.add, ALU.add,
                              waits=(s_pe.now(), s_ld.now()) + pl.g(("ft", tt)),
                              inc=s_dv)
                        pl.setg(("pb", bank), s_dv.now())
                        pl.setg(("fx", tt % 2), s_dv.now())
                    pl.dma("gpsimd", dchunk(dst_t, b, oc), tmpA, s_st,
                           waits=(s_dv.now(),))
                    for tt in range(NTT):
                        pl.setg(("ft", tt), s_st.now())
            pl.barrier()

    if upto >= 5:
        ffn_block(dX1, dX2)

    def proj_phase(wt, src_bufs, b, evac, gcol=None, preload=None):
        weight_to_bf(wt, D, gcol=gcol)
        if preload is not None:
            preload()
        pl.barrier()
        for blk8 in range(NCH):
            wcol = [wbf[ch][:, blk8 * 128:(blk8 + 1) * 128] for ch in range(NCH)]
            for tt in range(NTT):
                bank = tt % 2
                proj_ws(bank, wcol, src_bufs, tt)
                evac(blk8, tt, bank)

    if upto >= 6:
        sub = upto if 60 <= upto < 70 else 99
        for b in range(BPC):
            ser(4)
            ln_stats(dX2, b)
            rows_to_blk(b)
            ln_rowmath(b, "ln3")
            pl.barrier()
            if dbg_reps3 is not None:
                for j, rp in enumerate((repA, repB, repC)):
                    pl.dma("gpsimd", dap(dbg_reps3, (b * 3 + j) * 128 * T,
                                         [[T, 128], [1, T]]), rp, s_st,
                           waits=(s_ld.now(),))
                pl.barrier()
            for ch in range(NCH):
                i = ch % 2
                pl.dma("sync", xch[i], dchunk(dX2, b, ch), s_ld, waits=pl.g(("y3x", i)))
                tgt = tmpA if i == 0 else tmpB
                pl.op("gpsimd", "tensor_mul", tgt, xch[i], repA,
                      waits=(s_ld.now(),) + pl.g(("y3t", i)), inc=s_gp)
                pl.op("vector", "tensor_sub", xch[i], tgt, repB,
                      waits=(s_gp.now(),), inc=s_dv)
                pl.op("vector", "tensor_mul", ybf[ch], xch[i], repC, inc=s_dv)
                pl.dma("gpsimd", dchunk(dY3, b, ch), xch[i], s_st, waits=(s_dv.now(),))
                pl.dma("gpsimd", dap(dY3c, (b * NCH + ch) * 128 * T, [[T, 128], [1, T]]),
                       ybf[ch], s_st)
                pl.setg(("y3x", i), s_st.now())
                pl.setg(("y3t", i), s_st.now())
            pl.barrier()

            if sub < 61:
                continue
            def evac_lin(blk8, tt, bank):
                W = (s_pe.now(),)
                pl.op("vector", "tensor_scalar_add", qsil[0][:, tt * 512:(tt + 1) * 512],
                      pb[bank][:, 0:512], bcols[:, 0 * 8 + blk8:0 * 8 + blk8 + 1],
                      waits=W + (pl.g("linq0") if tt == 0 else ()), inc=s_dv)
                pl.op("scalar", "activation", qsil[1][:, tt * 512:(tt + 1) * 512],
                      pb[bank][:, 0:512], AF.Gelu,
                      bias=bcols[:, 0 * 8 + blk8:0 * 8 + blk8 + 1],
                      waits=W + (s_dv.now(),) + (pl.g("linq1") if tt == 0 else ()),
                      inc=s_ac)
                pl.setg(("pb", bank), s_dv.now(), s_ac.now())
                if tt == NTT - 1:
                    pl.dma("gpsimd",
                           dap(dLin, (b * NCH + blk8) * 128 * T, [[T, 128], [1, T]]),
                           qsil[0], s_st, waits=(s_dv.now(),))
                    pl.dma("gpsimd",
                           dap(dXgel, (b * NCH + blk8) * 128 * T, [[T, 128], [1, T]]),
                           qsil[1], s_st, waits=(s_ac.now(),))
                    pl.setg("linq0", s_st.now())
                    pl.setg("linq1", s_st.now())
            proj_phase(wlin_t, ybf, b, evac_lin, gcol=g1c)
            pl.barrier()

            if sub < 62:
                continue
            def preload_lin():
                for ch in range(NCH):
                    pl.dma("sync", ybf[ch],
                           dap(dLin, (b * NCH + ch) * 128 * T, [[T, 128], [1, T]]), s_ld)

            def evac_conv(blk8, tt, bank):
                pl.op("vector", "tensor_scalar_add", tmpA[:, tt * 512:(tt + 1) * 512],
                      pb[bank][:, 0:512], bcols[:, 1 * 8 + blk8:1 * 8 + blk8 + 1],
                      waits=(s_pe.now(),) + (pl.g("cvt") if tt == 0 else ()), inc=s_dv)
                pl.setg(("pb", bank), s_dv.now())
                if tt == NTT - 1:
                    pl.op("vector", "tensor_copy", qsil[0], tmpA,
                          waits=pl.g("cvq"), inc=s_dv)
                    pl.dma("gpsimd", dchunk(dXrg, b, blk8), tmpA, s_st,
                           waits=(s_dv.now(),))
                    pl.dma("gpsimd",
                           dap(dXrgb, (b * NCH + blk8) * 128 * T, [[T, 128], [1, T]]),
                           qsil[0], s_st)
                    pl.setg("cvt", s_st.now())
                    pl.setg("cvq", s_st.now())
            proj_phase(wconv_t, ybf, b, evac_conv, preload=preload_lin)
            pl.barrier()

            if sub < 63:
                continue
            xrgb_sb = ybf

            def preload_xrgb():
                for ch in range(NCH):
                    pl.dma("sync", xrgb_sb[ch],
                           dap(dXrgb, (b * NCH + ch) * 128 * T, [[T, 128], [1, T]]), s_ld)

            def evac_wa(blk8, tt, bank):
                pl.op("scalar", "activation", tmpA[:, tt * 512:(tt + 1) * 512],
                      pb[bank][:, 0:512], AF.Sigmoid,
                      bias=bcols[:, 3 * 8 + blk8:3 * 8 + blk8 + 1],
                      waits=(s_pe.now(),) + (pl.g("wat") if tt == 0 else ()), inc=s_ac)
                pl.setg(("pb", bank), s_ac.now())
                if tt == NTT - 1:
                    pl.op("vector", "tensor_scalar_mul", tmpB, tmpA,
                          logac[:, blk8:blk8 + 1],
                          waits=(s_ac.now(),) + pl.g("wab"), inc=s_dv)
                    pl.dma("gpsimd", dchunk(dLa, b, blk8), tmpB, s_st,
                           waits=(s_dv.now(),))
                    pl.setg("wat", s_st.now())
                    pl.setg("wab", s_st.now())
            proj_phase(wa_t, xrgb_sb, b, evac_wa, preload=preload_xrgb)
            pl.barrier()

            if sub < 64:
                continue
            def evac_wx(blk8, tt, bank):
                pl.op("scalar", "activation", tmpA[:, tt * 512:(tt + 1) * 512],
                      pb[bank][:, 0:512], AF.Sigmoid,
                      bias=bcols[:, 4 * 8 + blk8:4 * 8 + blk8 + 1],
                      waits=(s_pe.now(),) + (pl.g("wxt") if tt == 0 else ()), inc=s_ac)
                pl.setg(("pb", bank), s_ac.now())
                if tt == NTT - 1:
                    pl.dma("sync", xch[0], dchunk(dLa, b, blk8), s_ld,
                           waits=pl.g("rgx0"))
                    pl.dma("sync", xch[1], dchunk(dXrg, b, blk8), s_ld,
                           waits=pl.g("rgx1"))
                    pl.dma("sync", qsil[0],
                           dap(dXgel, (b * NCH + blk8) * 128 * T, [[T, 128], [1, T]]),
                           s_ld, waits=pl.g("rgq0"))
                    WL = (s_ld.now(), s_ac.now())
                    pl.op("scalar", "activation", sfp, xch[0], AF.Exp, waits=WL, inc=s_ac)
                    pl.setg("rgx0", s_ac.now())
                    pl.op("vector", "tensor_scalar_min", tmpB, sfp, 1.0,
                          waits=(s_ac.now(),) + pl.g("scrfT"), inc=s_dv)
                    pl.op("vector", "tensor_scalar", repA, tmpB, -1.0, 1.0,
                          ALU.mult, ALU.add, waits=pl.g("scrfA"), inc=s_dv)
                    pl.op("vector", "scalar_tensor_tensor", repB, tmpB, 1.0, repA,
                          ALU.add, ALU.mult, waits=pl.g("scrfB"), inc=s_dv)
                    pl.op("scalar", "activation", repA, repB, AF.Sqrt,
                          waits=(s_dv.now(),), inc=s_ac)
                    pl.op("vector", "tensor_mul", repB, tmpA, xch[1],
                          waits=(s_ac.now(),), inc=s_dv)
                    pl.setg("rgx1", s_dv.now())
                    pl.op("vector", "tensor_mul", tmpA, repA, repB,
                          waits=(s_ac.now(),), inc=s_dv)
                    pl.setg("scrfA", s_dv.now())
                    pl.op("vector", "tensor_tensor_scan", repB, tmpB, tmpA, 0.0,
                          ALU.mult, ALU.add, inc=s_dv)
                    pl.setg("wxt", s_dv.now())
                    pl.setg("scrfB", s_dv.now())
                    pl.setg("scrfT", s_dv.now())
                    pl.op("vector", "tensor_mul", qsil[1], qsil[0], repB,
                          waits=(s_ld.now(),) + pl.g("rgq1"), inc=s_dv)
                    pl.setg("rgq0", s_dv.now())
                    pl.dma("gpsimd",
                           dap(dXgh, (b * NCH + blk8) * 128 * T, [[T, 128], [1, T]]),
                           qsil[1], s_st, waits=(s_dv.now(),))
                    pl.setg("rgq1", s_st.now())
            proj_phase(wx_t, xrgb_sb, b, evac_wx)
            pl.barrier()

            if sub < 65:
                continue
            def preload_xgh():
                for ch in range(NCH):
                    pl.dma("sync", xrgb_sb[ch],
                           dap(dXgh, (b * NCH + ch) * 128 * T, [[T, 128], [1, T]]), s_ld)

            def evac_lin2(blk8, tt, bank):
                pl.dma("sync", xch[tt % 2][:, 0:512], dchunk(dY3, b, blk8, tt), s_ld,
                       waits=pl.g(("l2x", tt % 2)))
                pl.op("vector", "scalar_tensor_tensor", tmpA[:, tt * 512:(tt + 1) * 512],
                      pb[bank][:, 0:512], bcols[:, 2 * 8 + blk8:2 * 8 + blk8 + 1],
                      xch[tt % 2][:, 0:512], ALU.add, ALU.add,
                      waits=(s_pe.now(), s_ld.now()) + (pl.g("l2t") if tt == 0 else ()),
                      inc=s_dv)
                pl.setg(("pb", bank), s_dv.now())
                pl.setg(("l2x", tt % 2), s_dv.now())
                if tt == NTT - 1:
                    pl.dma("gpsimd", dchunk(dResB, b, blk8), tmpA, s_st,
                           waits=(s_dv.now(),))
                    pl.setg("l2t", s_st.now())
            proj_phase(wlin2_t, xrgb_sb, b, evac_lin2, preload=preload_xgh)
            pl.barrier()

            if sub < 66:
                continue
            ser(5)
            ln_stats(dResB, b, need=1)
            rows_to_blk(b)
            ln_rowmath(b, "rms")
            pl.barrier()
            for ch in range(NCH):
                i = ch % 2
                pl.dma("sync", xch[i], dchunk(dResB, b, ch), s_ld,
                       waits=pl.g(("xm", i)))
                pl.op("vector", "tensor_mul", ybf[ch], xch[i], repA,
                      waits=(s_ld.now(),), inc=s_dv)
                pl.setg(("xm", i), s_dv.now())
            pl.barrier()

            if sub < 67:
                continue
            def evac_w1(blk8, tt, bank):
                pl.op("scalar", "activation", tmpA[:, tt * 512:(tt + 1) * 512],
                      pb[bank][:, 0:512], AF.Sigmoid,
                      bias=bcols[:, 5 * 8 + blk8:5 * 8 + blk8 + 1],
                      waits=(s_pe.now(),) + (pl.g("w1t") if tt == 0 else ()), inc=s_ac)
                pl.setg(("pb", bank), s_ac.now())
                if tt == NTT - 1:
                    pl.op("scalar", "activation", qsil[0], tmpA, AF.Gelu,
                          waits=pl.g("w1q"), inc=s_ac)
                    pl.setg("w1t", s_ac.now())
                    pl.dma("gpsimd",
                           dap(dGate, (b * NCH + blk8) * 128 * T, [[T, 128], [1, T]]),
                           qsil[0], s_st, waits=(s_ac.now(),))
                    pl.setg("w1q", s_st.now())
            proj_phase(w1_t, ybf, b, evac_w1, gcol=g2c)
            pl.barrier()

            if sub < 68:
                continue
            def evac_w2(blk8, tt, bank):
                pl.op("vector", "tensor_scalar_add", tmpA[:, tt * 512:(tt + 1) * 512],
                      pb[bank][:, 0:512], bcols[:, 6 * 8 + blk8:6 * 8 + blk8 + 1],
                      waits=(s_pe.now(),) + (pl.g("w2t") if tt == 0 else ()), inc=s_dv)
                pl.setg(("pb", bank), s_dv.now())
                if tt == NTT - 1:
                    pl.dma("sync", qsil[0],
                           dap(dGate, (b * NCH + blk8) * 128 * T, [[T, 128], [1, T]]),
                           s_ld, waits=pl.g("w2g"))
                    pl.op("vector", "tensor_mul", qsil[1], tmpA, qsil[0],
                          waits=(s_ld.now(), s_dv.now()) + pl.g("w2q"), inc=s_dv)
                    pl.setg("w2t", s_dv.now())
                    pl.setg("w2g", s_dv.now())
                    pl.dma("gpsimd",
                           dap(dComb, (b * NCH + blk8) * 128 * T, [[T, 128], [1, T]]),
                           qsil[1], s_st, waits=(s_dv.now(),))
                    pl.setg("w2q", s_st.now())
            proj_phase(w2_t, ybf, b, evac_w2, gcol=g2c)
            pl.barrier()

            if sub < 69:
                continue
            def preload_comb():
                for ch in range(NCH):
                    pl.dma("sync", xrgb_sb[ch],
                           dap(dComb, (b * NCH + ch) * 128 * T, [[T, 128], [1, T]]), s_ld)

            def evac_w3(blk8, tt, bank):
                pl.dma("sync", xch[tt % 2][:, 0:512], dchunk(dResB, b, blk8, tt), s_ld,
                       waits=pl.g(("w3x", tt % 2)))
                pl.op("vector", "scalar_tensor_tensor", tmpA[:, tt * 512:(tt + 1) * 512],
                      pb[bank][:, 0:512], bcols[:, 7 * 8 + blk8:7 * 8 + blk8 + 1],
                      xch[tt % 2][:, 0:512], ALU.add, ALU.add,
                      waits=(s_pe.now(), s_ld.now()) + (pl.g("w3t") if tt == 0 else ()),
                      inc=s_dv)
                pl.setg(("pb", bank), s_dv.now())
                pl.dma("sync", xch[tt % 2][:, 0:512], dchunk(dX2, b, blk8, tt), s_ld,
                       waits=(s_dv.now(),))
                pl.op("vector", "tensor_add", tmpB[:, tt * 512:(tt + 1) * 512],
                      tmpA[:, tt * 512:(tt + 1) * 512], xch[tt % 2][:, 0:512],
                      waits=(s_ld.now(),) + (pl.g("w3tb") if tt == 0 else ()), inc=s_dv)
                pl.setg(("w3x", tt % 2), s_dv.now())
                if tt == NTT - 1:
                    pl.dma("gpsimd", dchunk(dX3, b, blk8), tmpB, s_st,
                           waits=(s_dv.now(),))
                    pl.setg("w3t", s_st.now())
                    pl.setg("w3tb", s_st.now())
            proj_phase(w3_t, xrgb_sb, b, evac_w3, preload=preload_comb)
            ser(9)
            pl.barrier()

    if upto >= 7:
        ffn_block(dX3, out_t)

    for tp, (src, dst) in tap_outs.items():
        n = int(np.prod(src.shape))
        step = 128 * T
        for j in range(0, n, step):
            w = min(step, n - j)
            pl.dma("sync", dap(dst, j, [[1, 1], [1, w]]), dap(src, j, [[1, 1], [1, w]]),
                   s_ld)
    pl.barrier(force=True)
    pl.add("sync", None, waits=tuple((s, s.n) for s in pl.dma_sems))

    pl.emit()
    return nc


def _prep_inputs(inputs):
    bf = ml_dtypes.bfloat16
    x = np.ascontiguousarray(np.asarray(inputs["L0_attn_a_hidden_states"], np.float32))
    G1R = np.asarray(inputs["L0_ffn_G1_R"], np.float32)
    G2R = np.asarray(inputs["L0_ffn_G2_R"], np.float32)
    G1I = np.asarray(inputs["L0_ffn_G1_I"], np.float32)
    G2I = np.asarray(inputs["L0_ffn_G2_I"], np.float32)

    def tfac(G1, G2):
        G1v = G1[0]
        G2v = G2[:, :, 0, :]
        t1 = np.empty((4, 512, 512), np.float32)
        t2 = np.empty((4, 512, 512), np.float32)
        for c in range(4):
            A = G1v[:, c, :]
            t1[c] = np.broadcast_to(A.T[:, None, :, None], (32, 16, 32, 16)).reshape(512, 512)
            B = G2v[c]
            t2[c] = np.broadcast_to(B.T[None, :, None, :], (32, 16, 32, 16)).reshape(512, 512)
        return np.ascontiguousarray(t1), np.ascontiguousarray(t2)

    t1r, t2r = tfac(G1R, G2R)
    t1i, t2i = tfac(G1I, G2I)

    tri = np.ascontiguousarray(np.triu(np.ones((128, 128), np.float32)))
    triq = np.ascontiguousarray(tri * np.float32(SCALE))
    i64f = np.ascontiguousarray(np.concatenate([np.eye(64, dtype=np.float32)] * 2, axis=0))
    i128b = np.ascontiguousarray(np.eye(128, dtype=np.float32).astype(bf))
    rmat = np.zeros((128, 128), np.float32)
    for kp in range(64):
        rmat[kp + 64, kp] = -1.0
        rmat[kp, kp + 64] = 1.0
    rmat = np.ascontiguousarray(rmat.astype(bf))
    onesb = np.ascontiguousarray(np.ones((128, 1), np.float32).astype(bf))

    f32 = lambda a: np.ascontiguousarray(np.asarray(a, np.float32))
    bcols = np.stack([f32(inputs[k]).reshape(NCH, 128, 1) for k in (
        "L0_attn_b_b_lin", "L0_attn_b_b_conv", "L0_attn_b_b_lin2", "L0_attn_b_ba",
        "L0_attn_b_bx", "L0_attn_b_b1", "L0_attn_b_b2", "L0_attn_b_b3")])

    shared = {
        "wq": f32(inputs["L0_attn_a_Wq"]), "wk": f32(inputs["L0_attn_a_Wk"]),
        "wv": f32(inputs["L0_attn_a_Wv"]), "ws": f32(inputs["L0_attn_a_Ws"]),
        "wg": f32(inputs["L0_attn_a_Wg"]), "wo": f32(inputs["L0_attn_a_Wo"]),
        "qcw": f32(inputs["L0_attn_a_qconv_w"])[:, 0, :].reshape(4, 128, 4).copy(),
        "kcw": f32(inputs["L0_attn_a_kconv_w"])[:, 0, :].reshape(4, 128, 4).copy(),
        "vcw": f32(inputs["L0_attn_a_vconv_w"])[:, 0, :].reshape(8, 128, 4).copy(),
        "gnw": np.ascontiguousarray(
            np.broadcast_to(f32(inputs["L0_attn_a_gnorm_w"]), (128, 256))),
        "wlin": f32(np.asarray(inputs["L0_attn_b_W_lin"]).T),
        "wconv": f32(np.asarray(inputs["L0_attn_b_W_conv"]).T),
        "wlin2": f32(np.asarray(inputs["L0_attn_b_W_lin2"]).T),
        "wa": f32(np.asarray(inputs["L0_attn_b_Wa"]).T),
        "wx": f32(np.asarray(inputs["L0_attn_b_Wx"]).T),
        "w1": f32(np.asarray(inputs["L0_attn_b_W1"]).T),
        "w2": f32(np.asarray(inputs["L0_attn_b_W2"]).T),
        "w3": f32(np.asarray(inputs["L0_attn_b_W3"]).T),
        "bcols": np.ascontiguousarray(bcols),
        "g1": f32(inputs["L0_attn_b_g1"]).reshape(NCH, 128, 1).copy(),
        "g2": f32(inputs["L0_attn_b_g2"]).reshape(NCH, 128, 1).copy(),
        "lam": f32(inputs["L0_attn_b_Lam"]).reshape(NCH, 128, 1).copy(),
        "t1r": t1r, "t2r": t2r, "t1i": t1i, "t2i": t2i,
        "fbr": f32(inputs["L0_ffn_bias_real"]).reshape(4, 128, 1).copy(),
        "fbi": f32(inputs["L0_ffn_bias_imag"]).reshape(4, 128, 1).copy(),
        "tri": tri, "triq": triq, "i64f": i64f, "i128b": i128b, "rmat": rmat,
        "onesb": onesb,
    }
    per_core = []
    for c in range(NCORE):
        xc = x[c * BPC:(c + 1) * BPC]
        xfm = np.ascontiguousarray(xc.transpose(0, 2, 1)).reshape(BPC, NCH, 128, T)
        m = dict(shared)
        m["x0"] = np.ascontiguousarray(xfm)
        per_core.append(m)
    return per_core


SER_MASK = 0


def kernel(**inputs):
    per_core = _prep_inputs(inputs)
    nc = build(ser_mask=SER_MASK)
    res = run_bass_kernel_spmd(nc, per_core, core_ids=list(range(NCORE)))
    outs = []
    for c in range(NCORE):
        o = np.asarray(res.results[c]["out"])
        o = o.reshape(BPC, D, T).transpose(0, 2, 1)
        outs.append(o)
    return np.ascontiguousarray(np.concatenate(outs, axis=0))
